# version 26
# baseline (speedup 1.0000x reference)
import math
from contextlib import ExitStack

import numpy as np
import ml_dtypes

import concourse.bass as bass
import concourse.mybir as mybir
from concourse.bass_utils import run_bass_kernel_spmd
from concourse.bass_core_rust import IndirectOffsetOnAxis

F32 = mybir.dt.float32
BF16 = mybir.dt.bfloat16
I32 = mybir.dt.int32
AF = mybir.ActivationFunctionType
ALU = mybir.AluOpType

R = 1024
D = 4096
DC = 32
NJ = 32
KVH = 8
EPS = 1e-5
ISQ = 1.0 / math.sqrt(128.0)

ENGS = ("pe", "act", "dve", "gp", "sy", "sc")
DMA_ENGS = ("gp", "sy", "sc")


class Prog:

    def __init__(self):
        self.lists = {e: [] for e in ENGS}
        self.lists["sc"] = self.lists["act"]
        self.idx = {e: 0 for e in ENGS}
        self.ctr = {}
        self.eng_ctr = {}
        self.hiwater = {e: {} for e in ENGS}
        self.w = {}
        self.r = {}
        self.pending = {e: [] for e in ENGS}

    def add_ctr(self, name, sem, step):
        self.ctr[name] = [sem, 0, step]

    def token(self, eng, cname=None):
        cname = cname or self.eng_ctr[eng]
        sem, n, _ = self.ctr[cname]
        return (eng, cname, n, self.idx[eng])

    def _emit_wait(self, eng, tok):
        teng, cname, val, tidx = tok
        if val <= 0:
            return
        if teng == eng and teng not in DMA_ENGS:
            if self.idx[eng] - tidx > 6:
                return
        sem = self.ctr[cname][0]
        hw = self.hiwater[eng]
        if hw.get(cname, -1) >= val:
            return
        hw[cname] = val
        self.lists[eng].append(lambda e, s=sem, v=val: e.wait_ge(s, v))
        self.idx[eng] += 1

    def op(self, eng, calls, reads=(), writes=(), inc=True, ctr=None, tokens=()):
        toks = list(tokens)
        for res in reads:
            t = self.w.get(res)
            if t is not None:
                toks.append(t)
        for res in writes:
            t = self.w.get(res)
            if t is not None:
                toks.append(t)
            toks.extend(self.r.get(res, ()))
        best = {}
        for t in toks:
            if t is None:
                continue
            k = (t[0], t[1])
            if k not in best or t[2] > best[k][2]:
                best[k] = t
        for t in best.values():
            self._emit_wait(eng, t)
        cname = ctr or self.eng_ctr[eng]
        for i, call in enumerate(calls):
            if i == len(calls) - 1 and inc:
                c = self.ctr[cname]
                c[1] += c[2]
                self.lists[eng].append(
                    lambda e, f=call, s=c[0], st=c[2]: f(e).then_inc(s, st))
            else:
                self.lists[eng].append(lambda e, f=call: f(e))
            self.idx[eng] += 1
        if inc:
            tok = self.token(eng, cname)
            for kind, res in self.pending[eng]:
                self.r.setdefault(res, []).append(tok)
            self.pending[eng] = []
            for res in writes:
                self.w[res] = tok
                self.r[res] = []
            for res in reads:
                self.r.setdefault(res, []).append(tok)
            return tok
        else:
            for res in writes:
                self.pending[eng].append(("w", res))
            for res in reads:
                self.pending[eng].append(("r", res))
            return None


def build():
    nc = bass.Bass("TRN2", target_bir_lowering=False)

    x_h = nc.dram_tensor("x", [R, D], F32, kind="ExternalInput")
    wq_h = nc.dram_tensor("wq", [D, D], F32, kind="ExternalInput")
    wo_h = nc.dram_tensor("wo", [D, D], F32, kind="ExternalInput")
    wf_h = nc.dram_tensor("wf", [D, D], F32, kind="ExternalInput")
    kvc_h = nc.dram_tensor("kvc", [1024, 2, 8, 16, 128], F32, kind="ExternalInput")
    btb_h = nc.dram_tensor("btb", [512, 1], I32, kind="ExternalInput")
    am_h = nc.dram_tensor("am", [128, DC * 64], F32, kind="ExternalInput")
    bm_h = nc.dram_tensor("bm", [D, 64], F32, kind="ExternalInput")
    gw_h = nc.dram_tensor("gw", [128, 128], F32, kind="ExternalInput")
    gb_h = nc.dram_tensor("gbrow", [1, 4], F32, kind="ExternalInput")
    bv_h = nc.dram_tensor("bvec", [128, 32], F32, kind="ExternalInput")
    cos_h = nc.dram_tensor("cosT", [128, R], F32, kind="ExternalInput")
    sin_h = nc.dram_tensor("sinT", [128, R], F32, kind="ExternalInput")
    idf_h = nc.dram_tensor("idf", [128, 128], F32, kind="ExternalInput")
    idb_h = nc.dram_tensor("idb", [128, 128], BF16, kind="ExternalInput")
    iota_h = nc.dram_tensor("iotac", [64, 1], F32, kind="ExternalInput")
    out_h = nc.dram_tensor("out", [R, D], F32, kind="ExternalOutput")
    spill_h = nc.dram_tensor("xnspill", [128, DC * R], BF16, kind="Internal")

    kv_flat = kvc_h.ap().rearrange("b s h t d -> b (s h t d)")

    P = Prog()
    gstk = ExitStack()
    sb = lambda n, s, d: gstk.enter_context(nc.sbuf_tensor(n, s, d))
    sem = lambda n: gstk.enter_context(nc.semaphore(n))

    s_pe = sem("s_pe"); s_act = sem("s_act"); s_dve = sem("s_dve")
    s_gp = sem("s_gp"); s_sy = sem("s_sy"); s_sq = sem("s_sq"); s_out = sem("s_out")
    for nm, smh, st in (("pe", s_pe, 1), ("act", s_act, 1), ("dve", s_dve, 1),
                        ("gp", s_gp, 16), ("sy", s_sy, 16), ("sc", s_sq, 16),
                        ("out", s_out, 16)):
        P.add_ctr(nm, smh, st)
    P.eng_ctr = {"pe": "pe", "act": "act", "dve": "dve", "gp": "gp",
                 "sy": "sy", "sc": "sc"}

    qctx = sb("qctx", [128, NJ * R], BF16)
    idf = sb("idf_s", [128, 128], F32)
    idb = sb("idb_s", [128, 128], BF16)
    ones_r = sb("ones_r", [1, 128], F32)
    ones_cf = sb("ones_cf", [128, 1], F32)
    ones_sq = sb("ones_sq", [128, 128], F32)
    iota_s = sb("iota_s", [64, 1], F32)
    bT = sb("bT", [128, 32], F32)
    btcols = sb("btcols", [128, 4], I32)
    u_m = sb("u_m", [64, R], BF16)
    gsc = sb("gsc", [128, 4], F32)
    gbrow = sb("gbrow_s", [1, 4], F32)
    stats = {n: sb(n, [128, 1], F32) for n in
             ("sumA", "sumB", "sqA", "sqB", "sums", "sumsq", "mean", "msq",
              "var", "t1", "rstd", "negmr", "ge01", "m01", "t01", "ge23",
              "m23", "t23", "gef", "dd", "thrc", "epsc")}

    def fam_ctr(name):
        key = "f_" + name
        if key not in P.ctr:
            P.add_ctr(key, gstk.enter_context(nc.semaphore(key)), 16)
        return key

    def dma(eng, out_ap, in_ap, reads, writes, family=None, tokens=()):
        fam = fam_ctr(family or writes[0])
        return P.op(eng, [lambda e, o=out_ap, i=in_ap: e.dma_start(out=o, in_=i)],
                    reads=reads, writes=writes, ctr=fam, tokens=tokens)

    def mm(calls, reads, writes, inc=True, tokens=()):
        return P.op("pe", calls, reads=reads, writes=writes, inc=inc, tokens=tokens)

    p0stk = ExitStack()
    sb0 = lambda n, s, d: p0stk.enter_context(nc.sbuf_tensor(n, s, d))
    xr = [sb0(f"xr{i}", [128, 2048], F32) for i in range(2)]
    xnf = [sb0(f"xnf{i}", [128, 2048], BF16) for i in range(2)]
    xnT = sb0("xnT", [128, DC * R], BF16)
    wq_t = [sb0(f"wq_t{i}", [128, D], BF16) for i in range(2)]
    AT = sb0("AT", [128, DC * 64], BF16)
    gwT = sb0("gwT", [128, 128], BF16)

    thr_row = sb0("thr_row", [1, R], F32)
    mask_bf = sb0("mask_bf", [64, R], BF16)

    p0ps = ExitStack()
    ps0 = lambda n, s, d: p0ps.enter_context(nc.psum_tensor(n, s, d))
    p_g = ps0("p_g", [128, 4], F32)
    p_thr = ps0("p_thr", [1, 128], F32)
    p_trx = [ps0(f"p_trx{i}", [128, 256], BF16) for i in range(2)]
    pqps = ExitStack()
    p_q = [pqps.enter_context(nc.psum_tensor(f"p_q{i}", [128, R], F32))
           for i in range(2)]

    dma("sy", idf[:, :], idf_h.ap(), [], ["idf"])
    dma("sy", idb[:, :], idb_h.ap(), [], ["idb"])
    dma("sy", iota_s[:, :], iota_h.ap(), [], ["iota"])
    dma("sy", bT[:, :], bv_h.ap(), [], ["bT"])
    dma("sy", gbrow[:, :], gb_h.ap(), [], ["gbrow"])
    for i in range(4):
        dma("sy", btcols[:, i:i + 1], btb_h.ap()[i * 128:(i + 1) * 128, 0:1],
            [], [f"btc{i}"])
    P.op("dve", [lambda e: e.memset(ones_r[:, :], 1.0)], [], ["ones_r"])
    P.op("dve", [lambda e: e.memset(stats["epsc"][:, :], EPS)], [], ["epsc"])
    P.op("dve", [lambda e: e.memset(ones_cf[:, :], 1.0)], [], ["ones_cf"])
    P.op("dve", [lambda e: e.memset(ones_sq[:, :], 1.0)], [], ["ones_sq"])

    dma("gp", gwT[:, :], gw_h.ap(), [], ["gwT"])
    dma("gp", AT[:, :], am_h.ap(), [], ["AT"])

    cosA = qctx[:, 30 * R:31 * R].bitcast(F32)
    sinA = qctx[:, 31 * R:32 * R].bitcast(F32)
    tA1 = qctx[:, 28 * R:29 * R].bitcast(F32)
    tA2 = qctx[:, 29 * R:30 * R].bitcast(F32)
    dma("sy", cosA, cos_h.ap()[:, 0:512], [], ["cosA"])
    dma("sy", sinA, sin_h.ap()[:, 0:512], [], ["sinA"])

    def qmm_half(j, half, xres):
        par = j % 2
        dma("gp", wq_t[par][:, :].rearrange("p (c n) -> p c n", c=DC),
            wq_h.ap()[:, j * 128:(j + 1) * 128].rearrange("(c p) n -> p c n", p=128),
            [], [f"wq{par}"])
        for dc in range(DC):
            mm([lambda e, par=par, dc=dc, half=half: e.matmul(
                p_q[par][:, half * 512:(half + 1) * 512],
                wq_t[par][:, dc * 128:(dc + 1) * 128],
                xnT[:, dc * R + half * 512: dc * R + (half + 1) * 512],
                start=(dc == 0), stop=(dc == DC - 1))],
               [f"wq{par}", xres], [f"p_q{par}"], inc=(dc == DC - 1))
        return par

    for r in range(8):
        xres = "xnTa" if r < 4 else "xnTb"
        for hf in range(2):
            par = (2 * r + hf) % 2
            dma("sy", xr[par][:, :],
                x_h.ap()[r * 128:(r + 1) * 128, hf * 2048:(hf + 1) * 2048],
                [], [f"xr{par}"])
            sq = stats["sqA" if hf == 0 else "sqB"]
            sm = stats["sumA" if hf == 0 else "sumB"]
            P.op("act", [lambda e, par=par, sq=sq: e.activation(
                xnf[par][:, :], xr[par][:, :], AF.Square, accum_out=sq[:, :])],
                [f"xr{par}"], [f"xnf{par}", f"sq{hf}"])
            P.op("act", [lambda e, par=par, sm=sm: e.activation(
                xnf[par][:, :], xr[par][:, :], AF.Identity, accum_out=sm[:, :])],
                [f"xr{par}"], [f"xnf{par}", f"sum{hf}"])
        st = stats
        P.op("dve", [lambda e: e.tensor_add(st["sums"][:, :], st["sumA"][:, :], st["sumB"][:, :])],
             ["sum0", "sum1"], ["sums"])
        P.op("dve", [lambda e: e.tensor_add(st["sumsq"][:, :], st["sqA"][:, :], st["sqB"][:, :])],
             ["sq0", "sq1"], ["sumsq"])
        P.op("dve", [lambda e: e.tensor_scalar_mul(st["mean"][:, :], st["sums"][:, :], 1.0 / D)],
             ["sums"], ["mean"])
        P.op("dve", [lambda e: e.tensor_mul(st["msq"][:, :], st["mean"][:, :], st["mean"][:, :])],
             ["mean"], ["msq"])
        P.op("dve", [lambda e: e.tensor_scalar_mul(st["var"][:, :], st["sumsq"][:, :], 1.0 / D)],
             ["sumsq"], ["var"])
        P.op("dve", [lambda e: e.tensor_sub(st["var"][:, :], st["var"][:, :], st["msq"][:, :])],
             ["msq", "var"], ["var"])
        P.op("act", [lambda e: e.activation(st["t1"][:, :], st["var"][:, :], AF.Ln, bias=st["epsc"][:, :])],
             ["var", "epsc"], ["t1"])
        P.op("act", [lambda e: e.activation(st["rstd"][:, :], st["t1"][:, :], AF.Exp, scale=-0.5)],
             ["t1"], ["rstd"])
        P.op("dve", [lambda e: e.scalar_tensor_tensor(
            st["negmr"][:, :], st["mean"][:, :], -1.0, st["rstd"][:, :],
            ALU.mult, ALU.mult)], ["mean", "rstd"], ["negmr"])
        for hf in range(2):
            par = (2 * r + hf) % 2
            P.op("act", [lambda e, par=par: e.activation(
                xnf[par][:, :], xr[par][:, :], AF.Identity,
                bias=st["negmr"][:, :], scale=st["rstd"][:, :])],
                [f"xr{par}", "negmr", "rstd"], [f"xnf{par}"])
            for k2 in range(8):
                dc0 = hf * 16 + 2 * k2
                tp = k2 % 2
                mm([
                    lambda e, par=par, k2=k2, tp=tp: e.transpose(
                        p_trx[tp][:, 0:128],
                        xnf[par][:, (2 * k2) * 128:(2 * k2 + 1) * 128], idb[:, :]),
                    lambda e, par=par, k2=k2, tp=tp: e.transpose(
                        p_trx[tp][:, 128:256],
                        xnf[par][:, (2 * k2 + 1) * 128:(2 * k2 + 2) * 128], idb[:, :]),
                ], [f"xnf{par}", "idb"], [f"p_trx{tp}"])
                P.op("dve", [
                    lambda e, dc0=dc0, r=r, tp=tp: e.tensor_copy(
                        xnT[:, dc0 * R + r * 128: dc0 * R + (r + 1) * 128],
                        p_trx[tp][:, 0:128]),
                    lambda e, dc0=dc0, r=r, tp=tp: e.tensor_copy(
                        xnT[:, (dc0 + 1) * R + r * 128: (dc0 + 1) * R + (r + 1) * 128],
                        p_trx[tp][:, 128:256]),
                ], [f"p_trx{tp}"], [xres])
                mm([
                    lambda e, dc0=dc0, r=r: e.matmul(
                        p_g[:, :], xnT[:, dc0 * R + r * 128: dc0 * R + (r + 1) * 128],
                        gwT[:, dc0 * 4:(dc0 + 1) * 4],
                        start=(dc0 == 0), stop=False),
                    lambda e, dc0=dc0, r=r: e.matmul(
                        p_g[:, :], xnT[:, (dc0 + 1) * R + r * 128: (dc0 + 1) * R + (r + 1) * 128],
                        gwT[:, (dc0 + 1) * 4:(dc0 + 2) * 4],
                        start=False, stop=False),
                ], [xres, "gwT"], ["p_g"], inc=False)
        mm([lambda e: e.matmul(p_g[:, :], ones_r[:, :], gbrow[:, :],
                               start=False, stop=True)],
           ["ones_r", "gbrow"], ["p_g"])
        P.op("dve", [lambda e: e.tensor_copy(gsc[:, :], p_g[:, :])], ["p_g"], ["gsc"])
        s0v, s1v, s2v, s3v = (gsc[:, i:i + 1] for i in range(4))
        P.op("dve", [lambda e, a=s0v, b=s1v: e.tensor_tensor(st["ge01"][:, :], a, b, ALU.is_ge)], ["gsc"], ["ge01"])
        P.op("dve", [lambda e, a=s0v, b=s1v: e.tensor_tensor(st["m01"][:, :], a, b, ALU.max)], ["gsc"], ["m01"])
        P.op("dve", [lambda e, a=s2v, b=s3v: e.tensor_tensor(st["ge23"][:, :], a, b, ALU.is_ge)], ["gsc"], ["ge23"])
        P.op("dve", [lambda e, a=s2v, b=s3v: e.tensor_tensor(st["m23"][:, :], a, b, ALU.max)], ["gsc"], ["m23"])
        P.op("dve", [lambda e: e.tensor_scalar(st["t01"][:, :], st["ge01"][:, :], -8.0, 16.0, ALU.mult, ALU.add)],
             ["ge01"], ["t01"])
        P.op("dve", [lambda e: e.tensor_scalar(st["t23"][:, :], st["ge23"][:, :], -32.0, 64.0, ALU.mult, ALU.add)],
             ["ge23"], ["t23"])
        P.op("dve", [lambda e: e.tensor_tensor(st["gef"][:, :], st["m01"][:, :], st["m23"][:, :], ALU.is_ge)],
             ["m01", "m23"], ["gef"])
        P.op("dve", [lambda e: e.tensor_sub(st["dd"][:, :], st["t01"][:, :], st["t23"][:, :])],
             ["t01", "t23"], ["dd"])
        P.op("dve", [lambda e: e.tensor_mul(st["dd"][:, :], st["dd"][:, :], st["gef"][:, :])],
             ["dd", "gef"], ["dd"])
        P.op("dve", [lambda e: e.tensor_add(st["thrc"][:, :], st["t23"][:, :], st["dd"][:, :])],
             ["t23", "dd"], ["thrc"])
        mm([lambda e: e.transpose(p_thr[:, :], st["thrc"][:, :], idf[:, :])],
           ["thrc", "idf"], ["p_thr"])
        P.op("dve", [lambda e, r=r: e.tensor_copy(thr_row[0:1, r * 128:(r + 1) * 128], p_thr[:, :])],
             ["p_thr"], ["thr_row"])
        if 3 <= r <= 6:
            for j in range(4 * (r - 3), 4 * (r - 3) + 4):
                par = qmm_half(j, 0, "xnTa")
                P.op("dve", [lambda e, par=par: e.tensor_mul(
                    tA1[:, :], p_q[par][:, 0:512], cosA)],
                    [f"p_q{par}", "cosA"], ["tA1"])
                P.op("dve", [
                    lambda e, par=par: e.tensor_mul(tA2[0:64, :], p_q[par][64:128, 0:512], sinA[0:64, :]),
                    lambda e, par=par: e.tensor_mul(tA2[64:128, :], p_q[par][0:64, 0:512], sinA[64:128, :]),
                ], [f"p_q{par}", "sinA"], ["tA2"])
                P.op("dve", [lambda e, j=j: e.tensor_add(
                    qctx[:, j * R: j * R + 512], tA1[:, :], tA2[:, :])],
                    ["tA1", "tA2", f"p_q{par}"], [f"qctx{j}"])

    p0_act = P.token("act")
    p0_dve = P.token("dve")

    dma("sy", spill_h.ap(), xnT[:, :], ["xnTa", "xnTb"], ["spill"])

    cosv = xnf[0][:, :].bitcast(F32)
    sinv = xnf[1][:, :].bitcast(F32)
    dma("sy", cosv, cos_h.ap(), [], ["xnf0"])
    dma("sy", sinv, sin_h.ap(), [], ["xnf1"])
    tmp1 = xr[0][:, 0:1024]
    tmp2 = xr[1][:, 0:1024]

    for j in range(16):
        par = qmm_half(j, 1, "xnTb")
        jr = j * R
        P.op("dve", [lambda e, par=par: e.tensor_mul(
            tmp1[:, 0:512], p_q[par][:, 512:1024], cosv[:, 512:1024])],
            [f"p_q{par}", "xnf0"], ["xr0"])
        P.op("dve", [
            lambda e, par=par: e.tensor_mul(tmp2[0:64, 0:512], p_q[par][64:128, 512:1024], sinv[0:64, 512:1024]),
            lambda e, par=par: e.tensor_mul(tmp2[64:128, 0:512], p_q[par][0:64, 512:1024], sinv[64:128, 512:1024]),
        ], [f"p_q{par}", "xnf1"], ["xr1"])
        P.op("dve", [lambda e, jr=jr: e.tensor_add(
            qctx[:, jr + 512:jr + R], tmp1[:, 0:512], tmp2[:, 0:512])],
            ["xr0", "xr1", f"p_q{par}"], [f"qctx{j}"])

    for j in range(16, NJ):
        par = j % 2
        dma("gp", wq_t[par][:, :].rearrange("p (c n) -> p c n", c=DC),
            wq_h.ap()[:, j * 128:(j + 1) * 128].rearrange("(c p) n -> p c n", p=128),
            [], [f"wq{par}"])
        for dc in range(DC):
            for half in range(2):
                last = dc == DC - 1 and half == 1
                mm([lambda e, par=par, dc=dc, half=half: e.matmul(
                    p_q[par][:, half * 512:(half + 1) * 512],
                    wq_t[par][:, dc * 128:(dc + 1) * 128],
                    xnT[:, dc * R + half * 512: dc * R + (half + 1) * 512],
                    start=(dc == 0), stop=(dc == DC - 1))],
                   [f"wq{par}", "xnTa" if half == 0 else "xnTb"],
                   [f"p_q{par}"], inc=last)
        P.op("dve", [lambda e, par=par: e.tensor_mul(tmp1[:, :], p_q[par][:, :], cosv)],
             [f"p_q{par}", "xnf0"], ["xr0"])
        P.op("dve", [
            lambda e, par=par: e.tensor_mul(tmp2[0:64, :], p_q[par][64:128, :], sinv[0:64, :]),
            lambda e, par=par: e.tensor_mul(tmp2[64:128, :], p_q[par][0:64, :], sinv[64:128, :]),
        ], [f"p_q{par}", "xnf1"], ["xr1"])
        P.op("dve", [lambda e, j=j: e.tensor_add(
            qctx[:, j * R:(j + 1) * R], tmp1[:, :], tmp2[:, :])],
            ["xr0", "xr1", f"p_q{par}"], [f"qctx{j}"])

    pqps.close()
    p0ps.close()
    p1aps = ExitStack()
    p_u = p1aps.enter_context(nc.psum_tensor("p_u", [64, R], F32))
    p_thrb = p1aps.enter_context(nc.psum_tensor("p_thrb", [64, R], F32))

    for dc in range(DC):
        for half in range(2):
            last = dc == DC - 1 and half == 1
            mm([lambda e, dc=dc, half=half: e.matmul(
                p_u[:, half * 512:(half + 1) * 512],
                AT[:, dc * 64:(dc + 1) * 64],
                xnT[:, dc * R + half * 512: dc * R + (half + 1) * 512],
                start=(dc == 0), stop=(dc == DC - 1))],
               ["AT", "xnTa" if half == 0 else "xnTb"], ["p_u"], inc=last,
               tokens=(p0_dve,) if dc == 0 else ())
    for half in range(2):
        mm([lambda e, half=half: e.matmul(
            p_thrb[:, half * 512:(half + 1) * 512], ones_r[0:1, 0:64],
            thr_row[0:1, half * 512:(half + 1) * 512], start=True, stop=True)],
           ["ones_r", "thr_row"], ["p_thrb"], inc=(half == 1), tokens=(p0_dve,))
    P.op("dve", [lambda e: e.tensor_scalar(mask_bf[:, :], p_thrb[:, :],
                                           iota_s[:, :], None, ALU.is_gt)],
         ["p_thrb", "iota"], ["mask_bf"])
    P.op("dve", [lambda e: e.tensor_tensor(u_m[:, :], p_u[:, :], mask_bf[:, :], ALU.mult)],
         ["p_u", "mask_bf"], ["u_m"])

    p1_pe = P.token("pe")
    p1_dve = P.token("dve")
    spill_tok = P.w["spill"]

    p1aps.close()
    p2ps = ExitStack()
    p_sc = [p2ps.enter_context(nc.psum_tensor(f"p_sc{i}", [128, R], F32)) for i in range(2)]
    p_ctx = [p2ps.enter_context(nc.psum_tensor(f"p_ctx{i}", [128, R], F32)) for i in range(2)]

    p0stk.close()
    p2stk = ExitStack()
    sb2 = lambda n, s, d: p2stk.enter_context(nc.sbuf_tensor(n, s, d))
    V_bf = [sb2(f"V{i}", [128, 8192], BF16) for i in range(2)]
    KT = [sb2(f"KT{i}", [128, 8192], BF16) for i in range(2)]
    kraw = [sb2(f"kraw{i}", [128, 2048], BF16) for i in range(2)]
    exp_bf = [sb2(f"exp{i}", [128, R], BF16) for i in range(6)]
    pair_bf = sb2("pair", [128, R], BF16)
    acc_f2 = [sb2(f"acc{i}", [128, R], F32) for i in range(2)]
    bc_s = sb2("bc_s", [128, R], F32)
    recip_bc = sb2("recip_bc", [128, R], F32)

    guard = [t for t in (p0_act, p1_pe, p1_dve, spill_tok) if t is not None]

    def gather(dst_ap, tile_i, elem_off, writes, tokens):
        P.op("gp", [lambda e, d=dst_ap, t=tile_i, eo=elem_off: e.indirect_dma_start(
            d, None, kv_flat, IndirectOffsetOnAxis(btcols[:, t:t + 1], axis=0),
            element_offset=eo)],
            reads=[f"btc{tile_i}"], writes=writes, tokens=tokens,
            ctr=fam_ctr(writes[0]))

    def gathers_V(hn):
        par = hn % 2
        toks = guard if hn <= 1 else ()
        for ti in range(4):
            gather(V_bf[par][:, ti * 2048:(ti + 1) * 2048], ti,
                   16384 + hn * 2048, [f"V{par}_{ti}"], toks)

    def gathers_K(hn, tiles):
        toks = guard if hn <= 1 else ()
        for ti in tiles:
            kp = ti % 2
            gather(kraw[kp][:, :], ti, hn * 2048, [f"kraw{kp}"], toks)

    def transposes_K(hn, tiles):
        for ti in tiles:
            par_n = hn % 2
            kp = ti % 2
            for c in range(16):
                tc_ = ti * 16 + c
                P.op("sy", [lambda e, kp=kp, c=c, par_n=par_n, tc_=tc_:
                            e.dma_start_transpose(
                                KT[par_n][:, tc_ * 128:(tc_ + 1) * 128],
                                kraw[kp][:, c * 128:(c + 1) * 128])],
                     reads=[f"kraw{kp}"], writes=[f"KT{par_n}_{ti}"],
                     ctr=fam_ctr(f"KT{par_n}_{ti}"))

    gathers_V(0)
    gathers_K(0, (0, 1))

    def transposes_K0(tiles):
        for ti in tiles:
            kp = ti % 2
            pcv = p_ctx[ti % 2][:, :].bitcast(BF16)
            for cq in range(4):
                mm([lambda e, kp=kp, cq=cq, i=i, pcv=pcv: e.transpose(
                    pcv[:, (cq * 4 + i) * 128:(cq * 4 + i + 1) * 128],
                    kraw[kp][:, (cq * 4 + i) * 128:(cq * 4 + i + 1) * 128],
                    idb[:, :]) for i in range(4)],
                   [f"kraw{kp}", "idb"], [f"p_ctx{ti % 2}"],
                   tokens=(p1_dve,) if ti < 2 and cq == 0 else ())
                P.op("dve", [lambda e, ti=ti, cq=cq, pcv=pcv: e.tensor_copy(
                    KT[0][:, (ti * 16 + cq * 4) * 128:(ti * 16 + cq * 4 + 4) * 128],
                    pcv[:, (cq * 4) * 128:(cq * 4 + 4) * 128])],
                    [f"p_ctx{ti % 2}"], [f"KT0_{ti}"])

    transposes_K0((0, 1))
    gathers_K(0, (2, 3))
    transposes_K0((2, 3))

    def tail_A(j):
        cp = j % 2
        P.op("dve", [lambda e, j=j, cp=cp: e.tensor_copy(
            qctx[:, j * R:(j + 1) * R], p_ctx[cp][:, :])],
            [f"p_ctx{cp}"], [f"qctx{j}"])

    def tail_B(j):
        ap = j % 2
        mm([
            lambda e, ap=ap: e.matmul(p_ctx[ap][:, 0:512], ones_sq[:, :],
                               acc_f2[ap][:, 0:512], start=True, stop=True),
            lambda e, ap=ap: e.matmul(p_ctx[ap][:, 512:1024], ones_sq[:, :],
                               acc_f2[ap][:, 512:1024], start=True, stop=True),
        ], [f"acc{ap}", "ones_sq"], [f"p_ctx{ap}"])

    def tail_C(j):
        cp = j % 2
        P.op("act", [lambda e, cp=cp: e.activation(bc_s[:, :], p_ctx[cp][:, :], AF.Ln)],
             [f"p_ctx{cp}"], ["bc_s"])
        P.op("act", [lambda e: e.activation(recip_bc[:, :], bc_s[:, :], AF.Exp,
                                            scale=-1.0)],
             ["bc_s"], ["recip_bc"])

    def tail_D(j):
        P.op("dve", [lambda e, j=j: e.tensor_tensor(
            qctx[:, j * R:(j + 1) * R], qctx[:, j * R:(j + 1) * R],
            recip_bc[:, :], ALU.mult)],
            ["recip_bc"], [f"qctx{j}"])

    for h in range(KVH):
        par = h % 2
        if h + 1 < KVH:
            gathers_V(h + 1)
            gathers_K(h + 1, (0, 1))
        for jl in range(4):
            j = h * 4 + jl
            jprev = j - 1
            if h + 1 < KVH and jl == 1:
                transposes_K(h + 1, (0, 1))
            if h + 1 < KVH and jl == 2:
                gathers_K(h + 1, (2, 3))
                transposes_K(h + 1, (2, 3))
            for t in range(66):
                if t < 64:
                    tc = t
                    spar = tc % 2
                    se = tc % 6
                    mm([
                        lambda e, par=par, spar=spar, tc=tc, j=j: e.matmul(
                            p_sc[spar][:, 0:512], KT[par][:, tc * 128:(tc + 1) * 128],
                            qctx[:, j * R: j * R + 512], start=True, stop=True),
                        lambda e, par=par, spar=spar, tc=tc, j=j: e.matmul(
                            p_sc[spar][:, 512:1024], KT[par][:, tc * 128:(tc + 1) * 128],
                            qctx[:, j * R + 512: (j + 1) * R], start=True, stop=True),
                    ], [f"KT{par}_{tc // 16}", f"qctx{j}"], [f"sc{spar}"],
                       tokens=(p1_dve,) if (h == 0 and jl == 0 and tc <= 1) else ())
                    if t == 8 and jprev >= 0:
                        tail_B(jprev)
                    if t == 12 and jprev >= 0:
                        tail_D(jprev)
                    P.op("act", [lambda e, spar=spar, se=se: e.activation(
                        exp_bf[se][:, :], p_sc[spar][:, :], AF.Exp, scale=ISQ)],
                        [f"sc{spar}"], [f"exp{se}"])
                    if t == 10 and jprev >= 0:
                        tail_C(jprev)
                    if tc % 2 == 1:
                        e0, e1 = (tc - 1) % 6, tc % 6
                        ap = j % 2
                        if tc == 1:
                            P.op("dve", [lambda e, e0=e0, e1=e1, ap=ap: e.tensor_add(
                                acc_f2[ap][:, :], exp_bf[e0][:, :], exp_bf[e1][:, :])],
                                [f"exp{e0}", f"exp{e1}"], [f"acc{ap}"])
                        else:
                            P.op("dve", [lambda e, e0=e0, e1=e1: e.tensor_add(
                                pair_bf[:, :], exp_bf[e0][:, :], exp_bf[e1][:, :])],
                                [f"exp{e0}", f"exp{e1}"], ["pair"])
                            P.op("dve", [lambda e, ap=ap: e.tensor_add(
                                acc_f2[ap][:, :], acc_f2[ap][:, :], pair_bf[:, :])],
                                ["pair", f"acc{ap}"], [f"acc{ap}"])
                if t >= 2:
                    tc = t - 2
                    se = tc % 6
                    cp = j % 2
                    mm([
                        lambda e, cp=cp, par=par, se=se, tc=tc: e.matmul(
                            p_ctx[cp][:, 0:512], V_bf[par][:, tc * 128:(tc + 1) * 128],
                            exp_bf[se][:, 0:512], start=(tc == 0), stop=(tc == 63)),
                        lambda e, cp=cp, par=par, se=se, tc=tc: e.matmul(
                            p_ctx[cp][:, 512:1024], V_bf[par][:, tc * 128:(tc + 1) * 128],
                            exp_bf[se][:, 512:1024], start=(tc == 0), stop=(tc == 63)),
                    ], [f"V{par}_{tc // 16}", f"exp{se}"], [f"p_ctx{cp}"],
                       tokens=(p1_dve,) if (h == 0 and jl == 0 and tc == 0) else ())
            tail_A(j)
        if h == 6:
            P.w["_h6pe"] = P.token("pe")
            P.w["_h6dve"] = P.token("dve")
            P.w["_h6act"] = P.token("act")
    tail_B(NJ - 1)
    tail_C(NJ - 1)
    tail_D(NJ - 1)

    p2_pe = P.token("pe")
    p2_act = P.token("act")
    p2_dve = P.token("dve")

    p2ps.close()
    p3ps = ExitStack()
    p_S = [p3ps.enter_context(nc.psum_tensor(f"p_S{i}", [128, R], F32)) for i in range(2)]
    p_tr3 = [p3ps.enter_context(nc.psum_tensor(f"p_tr3{i}", [128, 512], BF16)) for i in range(2)]
    p_o = [p3ps.enter_context(nc.psum_tensor(f"p_o{i}", [128, 128], F32)) for i in range(2)]

    p2stk.close()
    p3stk = ExitStack()
    sb3 = lambda n, s, d: p3stk.enter_context(nc.sbuf_tensor(n, s, d))
    xnT2 = sb3("xnT2", [128, DC * R], BF16)
    wo_t = [sb3(f"wo_t{i}", [128, D], BF16) for i in range(2)]
    wf_nat = [sb3(f"wfn{i}", [128, D], BF16) for i in range(2)]
    wfT = [sb3(f"wfT{i}", [128, D], BF16) for i in range(2)]
    BT_s = sb3("BT_s", [64, D], BF16)
    b_raw = [sb3(f"b_raw{i}", [128, 64], BF16) for i in range(2)]
    sT = [sb3(f"sT{i}", [128, 512], F32) for i in range(2)]
    xp = [sb3(f"xp{i}", [128, 128], F32) for i in range(4)]
    opc = [sb3(f"opc{i}", [128, 128], F32) for i in range(4)]

    guard3 = [p2_pe, p2_act, p2_dve]

    h6toks = [t for t in (P.w.get("_h6pe"), P.w.get("_h6dve"), P.w.get("_h6act")) if t]
    for piece, toks in ((0, h6toks), (2, h6toks), (1, guard3), (3, guard3)):
        dma("sy", xnT2[:, piece * 8192:(piece + 1) * 8192],
            spill_h.ap()[:, piece * 8192:(piece + 1) * 8192],
            ["spill"], ["xnT2"], family="xnT2", tokens=toks)

    for c in range(DC):
        par = c % 2
        dma("gp", b_raw[par][:, :], bm_h.ap()[c * 128:(c + 1) * 128, :],
            [], [f"b_raw{par}"], tokens=guard3 if c <= 1 else ())
        mm([lambda e, par=par: e.transpose(
            p_tr3[par][0:64, 0:128], b_raw[par][:, :], idb[:, :])],
           [f"b_raw{par}", "idb"], [f"p_tr3{par}"])
        P.op("dve", [lambda e, c=c, par=par: e.tensor_copy(
            BT_s[:, c * 128:(c + 1) * 128], p_tr3[par][0:64, 0:128])],
            [f"p_tr3{par}"], ["BT_s"])

    pend = None
    for oc in range(DC):
        par = oc % 2
        dma("gp", wo_t[par][:, :].rearrange("p (c n) -> p c n", c=DC),
            wo_h.ap()[:, oc * 128:(oc + 1) * 128].rearrange("(c p) n -> p c n", p=128),
            [], [f"wo{par}"], tokens=guard3 if oc <= 1 else ())
        dma("gp", wf_nat[par][:, :], wf_h.ap()[oc * 128:(oc + 1) * 128, :],
            [], [f"wfn{par}"], tokens=guard3 if oc <= 1 else ())
        for g in range(8):
            tp = g % 2
            mm([lambda e, par=par, g=g, i=i, tp=tp: e.transpose(
                p_tr3[tp][:, i * 128:(i + 1) * 128],
                wf_nat[par][:, (g * 4 + i) * 128:(g * 4 + i + 1) * 128],
                idb[:, :]) for i in range(4)],
               [f"wfn{par}", "idb"], [f"p_tr3{tp}"])
            if g % 2 == 0:
                P.op("dve", [lambda e, par=par, g=g, tp=tp: e.tensor_copy(
                    wfT[par][:, g * 512:(g + 1) * 512], p_tr3[tp][:, :])],
                    [f"p_tr3{tp}"], [f"wfT{par}"])
            else:
                P.op("act", [lambda e, par=par, g=g, tp=tp: e.copy(
                    wfT[par][:, g * 512:(g + 1) * 512], p_tr3[tp][:, :])],
                    [f"p_tr3{tp}"], [f"wfT{par}"])
        for half in range(2):
            sl = slice(half * 512, (half + 1) * 512)
            if pend is not None:
                poc, phalf = pend
                psp = phalf
                P.op("dve", [lambda e, poc=poc, phalf=phalf, psp=psp: e.tensor_scalar(
                    sT[psp][:, :], p_S[poc % 2][:, phalf * 512:(phalf + 1) * 512],
                    bT[:, poc:poc + 1], None, ALU.add)],
                    [f"p_S{poc % 2}h{phalf}", "bT"], [f"sT{psp}"])
                for q in range(4):
                    xb = (2 * phalf + q) % 4
                    dma("gp", xp[xb][:, :],
                        x_h.ap()[phalf * 512 + q * 128: phalf * 512 + (q + 1) * 128,
                                 poc * 128:(poc + 1) * 128],
                        [], [f"xp{xb}"])

            def outpiece(q):
                poc, phalf = pend
                psp = phalf
                tp = q % 2
                rv = phalf * 512 + q * 128
                xb = (2 * phalf + q) % 4
                mm([lambda e, psp=psp, q=q, tp=tp: e.transpose(
                    p_o[tp][:, :], sT[psp][:, q * 128:(q + 1) * 128], idf[:, :])],
                   [f"sT{psp}", "idf"], [f"p_o{tp}"])
                P.op("dve", [lambda e, tp=tp, xb=xb: e.tensor_add(
                    opc[xb][:, :], p_o[tp][:, :], xp[xb][:, :])],
                    [f"p_o{tp}", f"xp{xb}"], [f"opc{xb}"])
                P.op("sy", [lambda e, xb=xb, rv=rv, poc=poc: e.dma_start(
                    out=out_h.ap()[rv:rv + 128, poc * 128:(poc + 1) * 128],
                    in_=opc[xb][:, :])],
                    reads=[f"opc{xb}"], writes=[], ctr=fam_ctr(f"out{xb}"))

            for hc in range(DC):
                mm([lambda e, par=par, hc=hc, sl=sl, half=half: e.matmul(
                    p_S[par][:, sl], wo_t[par][:, hc * 128:(hc + 1) * 128],
                    qctx[:, hc * R + half * 512: hc * R + (half + 1) * 512],
                    start=(hc == 0), stop=False)],
                   [f"wo{par}", f"qctx{hc}"], [f"p_S{par}h{half}"], inc=False,
                   tokens=tuple(guard3) if (oc <= 1 and hc == 0) else ())
                if pend is not None and hc in (7, 14, 21, 28):
                    outpiece(hc // 7 - 1)
            for dc in range(DC):
                mm([lambda e, par=par, dc=dc, sl=sl, half=half: e.matmul(
                    p_S[par][:, sl], wfT[par][:, dc * 128:(dc + 1) * 128],
                    xnT2[:, dc * R + half * 512: dc * R + (half + 1) * 512],
                    start=False, stop=False)],
                   [f"wfT{par}", "xnT2"], [f"p_S{par}h{half}"], inc=False)
            mm([lambda e, par=par, oc=oc, sl=sl, half=half: e.matmul(
                p_S[par][:, sl], BT_s[:, oc * 128:(oc + 1) * 128],
                u_m[:, half * 512:(half + 1) * 512], start=False, stop=True)],
               ["BT_s", "u_m"], [f"p_S{par}h{half}"])
            pend = (oc, half)

    poc, phalf = pend
    psp = phalf
    P.op("dve", [lambda e: e.tensor_scalar(
        sT[psp][:, :], p_S[poc % 2][:, phalf * 512:(phalf + 1) * 512],
        bT[:, poc:poc + 1], None, ALU.add)],
        [f"p_S{poc % 2}h{phalf}", "bT"], [f"sT{psp}"])
    for q in range(4):
        tp = q % 2
        rv = phalf * 512 + q * 128
        xb = (2 * phalf + q) % 4
        dma("gp", xp[xb][:, :],
            x_h.ap()[rv:rv + 128, poc * 128:(poc + 1) * 128], [], [f"xp{xb}"])
        mm([lambda e, q=q, tp=tp: e.transpose(
            p_o[tp][:, :], sT[psp][:, q * 128:(q + 1) * 128], idf[:, :])],
           [f"sT{psp}", "idf"], [f"p_o{tp}"])
        P.op("dve", [lambda e, tp=tp, xb=xb: e.tensor_add(
            opc[xb][:, :], p_o[tp][:, :], xp[xb][:, :])],
            [f"p_o{tp}", f"xp{xb}"], [f"opc{xb}"])
        P.op("sy", [lambda e, xb=xb, rv=rv: e.dma_start(
            out=out_h.ap()[rv:rv + 128, poc * 128:(poc + 1) * 128],
            in_=opc[xb][:, :])],
            reads=[f"opc{xb}"], writes=[], ctr=fam_ctr(f"out{xb}"))

    for xb in range(4):
        key = "f_" + f"out{xb}"
        semh, val, _ = P.ctr[key]
        P.lists["sy"].append(lambda e, s=semh, v=val: e.wait_ge(s, v))

    with nc.Block() as block:
        @block.tensor
        def _(t):
            for f in P.lists["pe"]:
                f(t)

        @block.scalar
        def _(s):
            for f in P.lists["act"]:
                f(s)

        @block.vector
        def _(v):
            for f in P.lists["dve"]:
                f(v)

        @block.gpsimd
        def _(g):
            for f in P.lists["gp"]:
                f(g)

        @block.sync
        def _(sy):
            for f in P.lists["sy"]:
                f(sy)

    p3ps.close()
    p3stk.close()
    gstk.close()
    return nc


def kernel(L0_attn_hidden_states, L0_attn_Wq, L0_attn_Wo, L0_attn_kv_cache,
           L0_attn_block_table, L0_ffn_W, L0_ffn_b, L0_ffn_A, L0_ffn_B,
           L0_ffn_gate_w, L0_ffn_gate_b):
    x = np.ascontiguousarray(np.asarray(L0_attn_hidden_states, dtype=np.float32)).reshape(8192, 4096)
    wq = np.ascontiguousarray(np.asarray(L0_attn_Wq, dtype=np.float32))
    wo = np.ascontiguousarray(np.asarray(L0_attn_Wo, dtype=np.float32))
    wf = np.ascontiguousarray(np.asarray(L0_ffn_W, dtype=np.float32))
    kvc = np.ascontiguousarray(np.asarray(L0_attn_kv_cache, dtype=np.float32))
    btb = np.ascontiguousarray(np.asarray(L0_attn_block_table, dtype=np.int32).reshape(512, 1))
    am = np.ascontiguousarray(np.asarray(L0_ffn_A, dtype=np.float32)
                              .reshape(64, 32, 128).transpose(2, 1, 0).reshape(128, 2048))
    bm = np.ascontiguousarray(np.asarray(L0_ffn_B, dtype=np.float32))
    gw = np.ascontiguousarray(np.asarray(L0_ffn_gate_w, dtype=np.float32)
                              .reshape(4, 32, 128).transpose(2, 1, 0).reshape(128, 128))
    gb = np.ascontiguousarray(np.asarray(L0_ffn_gate_b, dtype=np.float32).reshape(1, 4))
    bv = np.ascontiguousarray(np.asarray(L0_ffn_b, dtype=np.float32).reshape(32, 128).T)

    idf = np.eye(128, dtype=np.float32)
    idb = np.eye(128, dtype=np.float32).astype(ml_dtypes.bfloat16)
    iota = np.arange(64, dtype=np.float32).reshape(64, 1)
    inv = 1.0 / 10000.0 ** (np.arange(0, 128, 2, dtype=np.float32) / 128.0)
    ang = np.arange(4096, dtype=np.float32)[:, None] * inv[None, :]
    emb = np.concatenate([ang, ang], axis=1)
    cos_t = np.ascontiguousarray(np.cos(emb).astype(np.float32).T)
    sin_t = np.sin(emb).astype(np.float32)
    sin_t[:, :64] *= -1.0
    sin_t = np.ascontiguousarray(sin_t.T)

    nc = build()
    in_maps = []
    for c in range(8):
        s0 = (c * R) % 4096
        in_maps.append({
            "x": np.ascontiguousarray(x[c * R:(c + 1) * R]),
            "wq": wq, "wo": wo, "wf": wf, "kvc": kvc, "btb": btb,
            "am": am, "bm": bm, "gw": gw, "gbrow": gb, "bvec": bv,
            "cosT": np.ascontiguousarray(cos_t[:, s0:s0 + R]),
            "sinT": np.ascontiguousarray(sin_t[:, s0:s0 + R]),
            "idf": idf, "idb": idb, "iotac": iota,
        })
    res = run_bass_kernel_spmd(nc, in_maps, core_ids=list(range(8)))
    out = np.concatenate([np.asarray(res.results[c]["out"]) for c in range(8)], axis=0)
    return out.reshape(2, 4096, 4096)


# version 27
# speedup vs baseline: 1.0629x; 1.0629x over previous
import math
from contextlib import ExitStack

import numpy as np
import ml_dtypes

import concourse.bass as bass
import concourse.mybir as mybir
from concourse.bass_utils import run_bass_kernel_spmd
from concourse.bass_core_rust import IndirectOffsetOnAxis

F32 = mybir.dt.float32
BF16 = mybir.dt.bfloat16
I32 = mybir.dt.int32
AF = mybir.ActivationFunctionType
ALU = mybir.AluOpType

R = 1024
D = 4096
DC = 32
NJ = 32
KVH = 8
EPS = 1e-5
ISQ = 1.0 / math.sqrt(128.0)

ENGS = ("pe", "act", "dve", "gp", "sy", "sc")
DMA_ENGS = ("gp", "sy", "sc")


class Prog:

    def __init__(self):
        self.lists = {e: [] for e in ENGS}
        self.lists["sc"] = self.lists["act"]
        self.idx = {e: 0 for e in ENGS}
        self.ctr = {}
        self.eng_ctr = {}
        self.hiwater = {e: {} for e in ENGS}
        self.w = {}
        self.r = {}
        self.pending = {e: [] for e in ENGS}

    def add_ctr(self, name, sem, step):
        self.ctr[name] = [sem, 0, step]

    def token(self, eng, cname=None):
        cname = cname or self.eng_ctr[eng]
        sem, n, _ = self.ctr[cname]
        return (eng, cname, n, self.idx[eng])

    def _emit_wait(self, eng, tok):
        teng, cname, val, tidx = tok
        if val <= 0:
            return
        if teng == eng and teng not in DMA_ENGS:
            if self.idx[eng] - tidx > 6:
                return
        sem = self.ctr[cname][0]
        hw = self.hiwater[eng]
        if hw.get(cname, -1) >= val:
            return
        hw[cname] = val
        self.lists[eng].append(lambda e, s=sem, v=val: e.wait_ge(s, v))
        self.idx[eng] += 1

    def op(self, eng, calls, reads=(), writes=(), inc=True, ctr=None, tokens=()):
        toks = list(tokens)
        for res in reads:
            t = self.w.get(res)
            if t is not None:
                toks.append(t)
        for res in writes:
            t = self.w.get(res)
            if t is not None:
                toks.append(t)
            toks.extend(self.r.get(res, ()))
        best = {}
        for t in toks:
            if t is None:
                continue
            k = (t[0], t[1])
            if k not in best or t[2] > best[k][2]:
                best[k] = t
        for t in best.values():
            self._emit_wait(eng, t)
        cname = ctr or self.eng_ctr[eng]
        for i, call in enumerate(calls):
            if i == len(calls) - 1 and inc:
                c = self.ctr[cname]
                c[1] += c[2]
                self.lists[eng].append(
                    lambda e, f=call, s=c[0], st=c[2]: f(e).then_inc(s, st))
            else:
                self.lists[eng].append(lambda e, f=call: f(e))
            self.idx[eng] += 1
        if inc:
            tok = self.token(eng, cname)
            for kind, res in self.pending[eng]:
                self.r.setdefault(res, []).append(tok)
            self.pending[eng] = []
            for res in writes:
                self.w[res] = tok
                self.r[res] = []
            for res in reads:
                self.r.setdefault(res, []).append(tok)
            return tok
        else:
            for res in writes:
                self.pending[eng].append(("w", res))
            for res in reads:
                self.pending[eng].append(("r", res))
            return None


def build():
    nc = bass.Bass("TRN2", target_bir_lowering=False)

    x_h = nc.dram_tensor("x", [R, D], F32, kind="ExternalInput")
    wq_h = nc.dram_tensor("wq", [D, D], F32, kind="ExternalInput")
    wo_h = nc.dram_tensor("wo", [D, D], F32, kind="ExternalInput")
    wf_h = nc.dram_tensor("wf", [D, D], F32, kind="ExternalInput")
    kvc_h = nc.dram_tensor("kvc", [1024, 2, 8, 16, 128], F32, kind="ExternalInput")
    btb_h = nc.dram_tensor("btb", [512, 1], I32, kind="ExternalInput")
    am_h = nc.dram_tensor("am", [128, DC * 64], F32, kind="ExternalInput")
    bm_h = nc.dram_tensor("bm", [D, 64], F32, kind="ExternalInput")
    gw_h = nc.dram_tensor("gw", [128, 128], F32, kind="ExternalInput")
    gb_h = nc.dram_tensor("gbrow", [1, 4], F32, kind="ExternalInput")
    bv_h = nc.dram_tensor("bvec", [128, 32], F32, kind="ExternalInput")
    cos_h = nc.dram_tensor("cosT", [128, R], F32, kind="ExternalInput")
    sin_h = nc.dram_tensor("sinT", [128, R], F32, kind="ExternalInput")
    idf_h = nc.dram_tensor("idf", [128, 128], F32, kind="ExternalInput")
    idb_h = nc.dram_tensor("idb", [128, 128], BF16, kind="ExternalInput")
    iota_h = nc.dram_tensor("iotac", [64, 1], F32, kind="ExternalInput")
    out_h = nc.dram_tensor("out", [R, D], F32, kind="ExternalOutput")
    spill_h = nc.dram_tensor("xnspill", [128, DC * R], BF16, kind="Internal")

    kv_flat = kvc_h.ap().rearrange("b s h t d -> b (s h t d)")

    P = Prog()
    gstk = ExitStack()
    sb = lambda n, s, d: gstk.enter_context(nc.sbuf_tensor(n, s, d))
    sem = lambda n: gstk.enter_context(nc.semaphore(n))

    s_pe = sem("s_pe"); s_act = sem("s_act"); s_dve = sem("s_dve")
    s_gp = sem("s_gp"); s_sy = sem("s_sy"); s_sq = sem("s_sq"); s_out = sem("s_out")
    for nm, smh, st in (("pe", s_pe, 1), ("act", s_act, 1), ("dve", s_dve, 1),
                        ("gp", s_gp, 16), ("sy", s_sy, 16), ("sc", s_sq, 16),
                        ("out", s_out, 16)):
        P.add_ctr(nm, smh, st)
    P.eng_ctr = {"pe": "pe", "act": "act", "dve": "dve", "gp": "gp",
                 "sy": "sy", "sc": "sc"}

    qctx = sb("qctx", [128, NJ * R], BF16)
    idf = sb("idf_s", [128, 128], F32)
    idb = sb("idb_s", [128, 128], BF16)
    ones_r = sb("ones_r", [1, 128], F32)
    ones_cf = sb("ones_cf", [128, 1], F32)
    ones_sq = sb("ones_sq", [128, 128], F32)
    iota_s = sb("iota_s", [64, 1], F32)
    bT = sb("bT", [128, 32], F32)
    btcols = sb("btcols", [128, 4], I32)
    u_m = sb("u_m", [64, R], BF16)
    gsc = sb("gsc", [128, 4], F32)
    gbrow = sb("gbrow_s", [1, 4], F32)
    stats = {n: sb(n, [128, 1], F32) for n in
             ("sumA", "sumB", "sqA", "sqB", "sums", "sumsq", "mean", "msq",
              "var", "t1", "rstd", "negmr", "ge01", "m01", "t01", "ge23",
              "m23", "t23", "gef", "dd", "thrc", "epsc")}

    def fam_ctr(name):
        key = "f_" + name
        if key not in P.ctr:
            P.add_ctr(key, gstk.enter_context(nc.semaphore(key)), 16)
        return key

    def dma(eng, out_ap, in_ap, reads, writes, family=None, tokens=()):
        fam = fam_ctr(family or writes[0])
        return P.op(eng, [lambda e, o=out_ap, i=in_ap: e.dma_start(out=o, in_=i)],
                    reads=reads, writes=writes, ctr=fam, tokens=tokens)

    def mm(calls, reads, writes, inc=True, tokens=()):
        return P.op("pe", calls, reads=reads, writes=writes, inc=inc, tokens=tokens)

    p0stk = ExitStack()
    sb0 = lambda n, s, d: p0stk.enter_context(nc.sbuf_tensor(n, s, d))
    xr = [sb0(f"xr{i}", [128, 2048], F32) for i in range(2)]
    xnf = [sb0(f"xnf{i}", [128, 2048], BF16) for i in range(2)]
    xnT = sb0("xnT", [128, DC * R], BF16)
    wq_t = [sb0(f"wq_t{i}", [128, D], BF16) for i in range(2)]
    AT = sb0("AT", [128, DC * 64], BF16)
    gwT = sb0("gwT", [128, 128], BF16)

    thr_row = sb0("thr_row", [1, R], F32)
    mask_bf = sb0("mask_bf", [64, R], BF16)

    p0ps = ExitStack()
    ps0 = lambda n, s, d: p0ps.enter_context(nc.psum_tensor(n, s, d))
    p_g = ps0("p_g", [128, 4], F32)
    p_thr = ps0("p_thr", [1, 128], F32)
    p_trx = [ps0(f"p_trx{i}", [128, 256], BF16) for i in range(2)]

    dma("sy", idf[:, :], idf_h.ap(), [], ["idf"])
    dma("sy", idb[:, :], idb_h.ap(), [], ["idb"])
    dma("sy", iota_s[:, :], iota_h.ap(), [], ["iota"])
    dma("sy", bT[:, :], bv_h.ap(), [], ["bT"])
    dma("sy", gbrow[:, :], gb_h.ap(), [], ["gbrow"])
    for i in range(4):
        dma("sy", btcols[:, i:i + 1], btb_h.ap()[i * 128:(i + 1) * 128, 0:1],
            [], [f"btc{i}"])
    P.op("dve", [lambda e: e.memset(ones_r[:, :], 1.0)], [], ["ones_r"])
    P.op("dve", [lambda e: e.memset(stats["epsc"][:, :], EPS)], [], ["epsc"])
    P.op("dve", [lambda e: e.memset(ones_cf[:, :], 1.0)], [], ["ones_cf"])
    P.op("dve", [lambda e: e.memset(ones_sq[:, :], 1.0)], [], ["ones_sq"])

    dma("gp", gwT[:, :], gw_h.ap(), [], ["gwT"])
    dma("gp", AT[:, :], am_h.ap(), [], ["AT"])

    for r in range(8):
        for hf in range(2):
            par = (2 * r + hf) % 2
            dma("sy", xr[par][:, :],
                x_h.ap()[r * 128:(r + 1) * 128, hf * 2048:(hf + 1) * 2048],
                [], [f"xr{par}"])
            sq = stats["sqA" if hf == 0 else "sqB"]
            sm = stats["sumA" if hf == 0 else "sumB"]
            P.op("act", [lambda e, par=par, sq=sq: e.activation(
                xnf[par][:, :], xr[par][:, :], AF.Square, accum_out=sq[:, :])],
                [f"xr{par}"], [f"xnf{par}", f"sq{hf}"])
            P.op("act", [lambda e, par=par, sm=sm: e.activation(
                xnf[par][:, :], xr[par][:, :], AF.Identity, accum_out=sm[:, :])],
                [f"xr{par}"], [f"xnf{par}", f"sum{hf}"])
        st = stats
        P.op("dve", [lambda e: e.tensor_add(st["sums"][:, :], st["sumA"][:, :], st["sumB"][:, :])],
             ["sum0", "sum1"], ["sums"])
        P.op("dve", [lambda e: e.tensor_add(st["sumsq"][:, :], st["sqA"][:, :], st["sqB"][:, :])],
             ["sq0", "sq1"], ["sumsq"])
        P.op("dve", [lambda e: e.tensor_scalar_mul(st["mean"][:, :], st["sums"][:, :], 1.0 / D)],
             ["sums"], ["mean"])
        P.op("dve", [lambda e: e.tensor_mul(st["msq"][:, :], st["mean"][:, :], st["mean"][:, :])],
             ["mean"], ["msq"])
        P.op("dve", [lambda e: e.tensor_scalar_mul(st["var"][:, :], st["sumsq"][:, :], 1.0 / D)],
             ["sumsq"], ["var"])
        P.op("dve", [lambda e: e.tensor_sub(st["var"][:, :], st["var"][:, :], st["msq"][:, :])],
             ["msq", "var"], ["var"])
        P.op("act", [lambda e: e.activation(st["t1"][:, :], st["var"][:, :], AF.Ln, bias=st["epsc"][:, :])],
             ["var", "epsc"], ["t1"])
        P.op("act", [lambda e: e.activation(st["rstd"][:, :], st["t1"][:, :], AF.Exp, scale=-0.5)],
             ["t1"], ["rstd"])
        P.op("dve", [lambda e: e.scalar_tensor_tensor(
            st["negmr"][:, :], st["mean"][:, :], -1.0, st["rstd"][:, :],
            ALU.mult, ALU.mult)], ["mean", "rstd"], ["negmr"])
        for hf in range(2):
            par = (2 * r + hf) % 2
            P.op("act", [lambda e, par=par: e.activation(
                xnf[par][:, :], xr[par][:, :], AF.Identity,
                bias=st["negmr"][:, :], scale=st["rstd"][:, :])],
                [f"xr{par}", "negmr", "rstd"], [f"xnf{par}"])
            for k2 in range(8):
                dc0 = hf * 16 + 2 * k2
                tp = k2 % 2
                mm([
                    lambda e, par=par, k2=k2, tp=tp: e.transpose(
                        p_trx[tp][:, 0:128],
                        xnf[par][:, (2 * k2) * 128:(2 * k2 + 1) * 128], idb[:, :]),
                    lambda e, par=par, k2=k2, tp=tp: e.transpose(
                        p_trx[tp][:, 128:256],
                        xnf[par][:, (2 * k2 + 1) * 128:(2 * k2 + 2) * 128], idb[:, :]),
                ], [f"xnf{par}", "idb"], [f"p_trx{tp}"])
                P.op("dve", [
                    lambda e, dc0=dc0, r=r, tp=tp: e.tensor_copy(
                        xnT[:, dc0 * R + r * 128: dc0 * R + (r + 1) * 128],
                        p_trx[tp][:, 0:128]),
                    lambda e, dc0=dc0, r=r, tp=tp: e.tensor_copy(
                        xnT[:, (dc0 + 1) * R + r * 128: (dc0 + 1) * R + (r + 1) * 128],
                        p_trx[tp][:, 128:256]),
                ], [f"p_trx{tp}"], ["xnT"])
                mm([
                    lambda e, dc0=dc0, r=r: e.matmul(
                        p_g[:, :], xnT[:, dc0 * R + r * 128: dc0 * R + (r + 1) * 128],
                        gwT[:, dc0 * 4:(dc0 + 1) * 4],
                        start=(dc0 == 0), stop=False),
                    lambda e, dc0=dc0, r=r: e.matmul(
                        p_g[:, :], xnT[:, (dc0 + 1) * R + r * 128: (dc0 + 1) * R + (r + 1) * 128],
                        gwT[:, (dc0 + 1) * 4:(dc0 + 2) * 4],
                        start=False, stop=False),
                ], ["xnT", "gwT"], ["p_g"], inc=False)
        mm([lambda e: e.matmul(p_g[:, :], ones_r[:, :], gbrow[:, :],
                               start=False, stop=True)],
           ["ones_r", "gbrow"], ["p_g"])
        P.op("dve", [lambda e: e.tensor_copy(gsc[:, :], p_g[:, :])], ["p_g"], ["gsc"])
        s0v, s1v, s2v, s3v = (gsc[:, i:i + 1] for i in range(4))
        P.op("dve", [lambda e, a=s0v, b=s1v: e.tensor_tensor(st["ge01"][:, :], a, b, ALU.is_ge)], ["gsc"], ["ge01"])
        P.op("dve", [lambda e, a=s0v, b=s1v: e.tensor_tensor(st["m01"][:, :], a, b, ALU.max)], ["gsc"], ["m01"])
        P.op("dve", [lambda e, a=s2v, b=s3v: e.tensor_tensor(st["ge23"][:, :], a, b, ALU.is_ge)], ["gsc"], ["ge23"])
        P.op("dve", [lambda e, a=s2v, b=s3v: e.tensor_tensor(st["m23"][:, :], a, b, ALU.max)], ["gsc"], ["m23"])
        P.op("dve", [lambda e: e.tensor_scalar(st["t01"][:, :], st["ge01"][:, :], -8.0, 16.0, ALU.mult, ALU.add)],
             ["ge01"], ["t01"])
        P.op("dve", [lambda e: e.tensor_scalar(st["t23"][:, :], st["ge23"][:, :], -32.0, 64.0, ALU.mult, ALU.add)],
             ["ge23"], ["t23"])
        P.op("dve", [lambda e: e.tensor_tensor(st["gef"][:, :], st["m01"][:, :], st["m23"][:, :], ALU.is_ge)],
             ["m01", "m23"], ["gef"])
        P.op("dve", [lambda e: e.tensor_sub(st["dd"][:, :], st["t01"][:, :], st["t23"][:, :])],
             ["t01", "t23"], ["dd"])
        P.op("dve", [lambda e: e.tensor_mul(st["dd"][:, :], st["dd"][:, :], st["gef"][:, :])],
             ["dd", "gef"], ["dd"])
        P.op("dve", [lambda e: e.tensor_add(st["thrc"][:, :], st["t23"][:, :], st["dd"][:, :])],
             ["t23", "dd"], ["thrc"])
        mm([lambda e: e.transpose(p_thr[:, :], st["thrc"][:, :], idf[:, :])],
           ["thrc", "idf"], ["p_thr"])
        P.op("dve", [lambda e, r=r: e.tensor_copy(thr_row[0:1, r * 128:(r + 1) * 128], p_thr[:, :])],
             ["p_thr"], ["thr_row"])

    p0_act = P.token("act")
    p0_dve = P.token("dve")

    p0ps.close()
    p1aps = ExitStack()
    p_u = p1aps.enter_context(nc.psum_tensor("p_u", [64, R], F32))
    p_thrb = p1aps.enter_context(nc.psum_tensor("p_thrb", [64, R], F32))

    for dc in range(DC):
        for half in range(2):
            last = dc == DC - 1 and half == 1
            mm([lambda e, dc=dc, half=half: e.matmul(
                p_u[:, half * 512:(half + 1) * 512],
                AT[:, dc * 64:(dc + 1) * 64],
                xnT[:, dc * R + half * 512: dc * R + (half + 1) * 512],
                start=(dc == 0), stop=(dc == DC - 1))],
               ["AT", "xnT"], ["p_u"], inc=last,
               tokens=(p0_dve,) if dc == 0 else ())
    for half in range(2):
        mm([lambda e, half=half: e.matmul(
            p_thrb[:, half * 512:(half + 1) * 512], ones_r[0:1, 0:64],
            thr_row[0:1, half * 512:(half + 1) * 512], start=True, stop=True)],
           ["ones_r", "thr_row"], ["p_thrb"], inc=(half == 1), tokens=(p0_dve,))
    P.op("dve", [lambda e: e.tensor_scalar(mask_bf[:, :], p_thrb[:, :],
                                           iota_s[:, :], None, ALU.is_gt)],
         ["p_thrb", "iota"], ["mask_bf"])
    P.op("dve", [lambda e: e.tensor_tensor(u_m[:, :], p_u[:, :], mask_bf[:, :], ALU.mult)],
         ["p_u", "mask_bf"], ["u_m"])
    g_p1a = P.token("dve")

    dma("sy", spill_h.ap(), xnT[:, :], ["xnT"], ["spill"])

    cosv = xnf[0][:, :].bitcast(F32)
    sinv = xnf[1][:, :].bitcast(F32)
    dma("sy", cosv, cos_h.ap(), [], ["xnf0"])
    dma("sy", sinv, sin_h.ap(), [], ["xnf1"])
    tmp1 = xr[0][:, 0:1024]
    tmp2 = xr[1][:, 0:1024]

    p1aps.close()
    p1bps = ExitStack()
    p_q = [p1bps.enter_context(nc.psum_tensor(f"p_q{i}", [128, R], F32))
           for i in range(2)]

    for j in range(NJ):
        par = j % 2
        dma("gp", wq_t[par][:, :], wq_h.ap()[j * 128:(j + 1) * 128, :],
            [], [f"wq{par}"])
        for dc in range(DC):
            for half in range(2):
                last = dc == DC - 1 and half == 1
                mm([lambda e, par=par, dc=dc, half=half: e.matmul(
                    p_q[par][:, half * 512:(half + 1) * 512],
                    wq_t[par][:, dc * 128:(dc + 1) * 128],
                    xnT[:, dc * R + half * 512: dc * R + (half + 1) * 512],
                    start=(dc == 0), stop=(dc == DC - 1))],
                   [f"wq{par}", "xnT"], [f"p_q{par}"], inc=last,
                   tokens=(g_p1a,) if (j <= 1 and dc == 0) else ())
        P.op("dve", [lambda e, par=par: e.tensor_mul(tmp1[:, :], p_q[par][:, :], cosv)],
             [f"p_q{par}", "xnf0"], ["xr0"])
        P.op("dve", [
            lambda e, par=par: e.tensor_mul(tmp2[0:64, :], p_q[par][64:128, :], sinv[0:64, :]),
            lambda e, par=par: e.tensor_mul(tmp2[64:128, :], p_q[par][0:64, :], sinv[64:128, :]),
        ], [f"p_q{par}", "xnf1"], ["xr1"])
        P.op("dve", [lambda e, j=j: e.tensor_add(
            qctx[:, j * R:(j + 1) * R], tmp1[:, :], tmp2[:, :])],
            ["xr0", "xr1", f"p_q{par}"], [f"qctx{j}"])

    p1_pe = P.token("pe")
    p1_dve = P.token("dve")
    spill_tok = P.w["spill"]

    p1bps.close()
    p2ps = ExitStack()
    p_sc = [p2ps.enter_context(nc.psum_tensor(f"p_sc{i}", [128, R], F32)) for i in range(2)]
    p_ctx = [p2ps.enter_context(nc.psum_tensor(f"p_ctx{i}", [128, R], F32)) for i in range(2)]

    p0stk.close()
    p2stk = ExitStack()
    sb2 = lambda n, s, d: p2stk.enter_context(nc.sbuf_tensor(n, s, d))
    V_bf = [sb2(f"V{i}", [128, 8192], BF16) for i in range(2)]
    KT = [sb2(f"KT{i}", [128, 8192], BF16) for i in range(2)]
    kraw = [sb2(f"kraw{i}", [128, 2048], BF16) for i in range(2)]
    exp_bf = [sb2(f"exp{i}", [128, R], BF16) for i in range(6)]
    pair_bf = sb2("pair", [128, R], BF16)
    acc_f2 = [sb2(f"acc{i}", [128, R], F32) for i in range(2)]
    bc_s = sb2("bc_s", [128, R], F32)
    recip_bc = sb2("recip_bc", [128, R], F32)

    guard = [t for t in (p0_act, p1_pe, p1_dve, spill_tok) if t is not None]

    def gather(dst_ap, tile_i, elem_off, writes, tokens):
        P.op("gp", [lambda e, d=dst_ap, t=tile_i, eo=elem_off: e.indirect_dma_start(
            d, None, kv_flat, IndirectOffsetOnAxis(btcols[:, t:t + 1], axis=0),
            element_offset=eo)],
            reads=[f"btc{tile_i}"], writes=writes, tokens=tokens,
            ctr=fam_ctr(writes[0]))

    def gathers_V(hn):
        par = hn % 2
        toks = guard if hn <= 1 else ()
        for ti in range(4):
            gather(V_bf[par][:, ti * 2048:(ti + 1) * 2048], ti,
                   16384 + hn * 2048, [f"V{par}_{ti}"], toks)

    def gathers_K(hn, tiles):
        toks = guard if hn <= 1 else ()
        for ti in tiles:
            kp = ti % 2
            gather(kraw[kp][:, :], ti, hn * 2048, [f"kraw{kp}"], toks)

    def transposes_K(hn, tiles):
        for ti in tiles:
            par_n = hn % 2
            kp = ti % 2
            for c in range(16):
                tc_ = ti * 16 + c
                P.op("sy", [lambda e, kp=kp, c=c, par_n=par_n, tc_=tc_:
                            e.dma_start_transpose(
                                KT[par_n][:, tc_ * 128:(tc_ + 1) * 128],
                                kraw[kp][:, c * 128:(c + 1) * 128])],
                     reads=[f"kraw{kp}"], writes=[f"KT{par_n}_{ti}"],
                     ctr=fam_ctr(f"KT{par_n}_{ti}"))

    gathers_V(0)
    gathers_K(0, (0, 1))

    def transposes_K0(tiles):
        for ti in tiles:
            kp = ti % 2
            pcv = p_ctx[ti % 2][:, :].bitcast(BF16)
            for cq in range(4):
                mm([lambda e, kp=kp, cq=cq, i=i, pcv=pcv: e.transpose(
                    pcv[:, (cq * 4 + i) * 128:(cq * 4 + i + 1) * 128],
                    kraw[kp][:, (cq * 4 + i) * 128:(cq * 4 + i + 1) * 128],
                    idb[:, :]) for i in range(4)],
                   [f"kraw{kp}", "idb"], [f"p_ctx{ti % 2}"],
                   tokens=(p1_dve,) if ti < 2 and cq == 0 else ())
                P.op("dve", [lambda e, ti=ti, cq=cq, pcv=pcv: e.tensor_copy(
                    KT[0][:, (ti * 16 + cq * 4) * 128:(ti * 16 + cq * 4 + 4) * 128],
                    pcv[:, (cq * 4) * 128:(cq * 4 + 4) * 128])],
                    [f"p_ctx{ti % 2}"], [f"KT0_{ti}"])

    transposes_K0((0, 1))
    gathers_K(0, (2, 3))
    transposes_K0((2, 3))

    def tail_A(j):
        cp = j % 2
        P.op("dve", [lambda e, j=j, cp=cp: e.tensor_copy(
            qctx[:, j * R:(j + 1) * R], p_ctx[cp][:, :])],
            [f"p_ctx{cp}"], [f"qctx{j}"])

    def tail_B(j):
        ap = j % 2
        mm([
            lambda e, ap=ap: e.matmul(p_ctx[ap][:, 0:512], ones_sq[:, :],
                               acc_f2[ap][:, 0:512], start=True, stop=True),
            lambda e, ap=ap: e.matmul(p_ctx[ap][:, 512:1024], ones_sq[:, :],
                               acc_f2[ap][:, 512:1024], start=True, stop=True),
        ], [f"acc{ap}", "ones_sq"], [f"p_ctx{ap}"])

    def tail_C(j):
        cp = j % 2
        P.op("act", [lambda e, cp=cp: e.activation(bc_s[:, :], p_ctx[cp][:, :], AF.Ln)],
             [f"p_ctx{cp}"], ["bc_s"])
        P.op("act", [lambda e: e.activation(recip_bc[:, :], bc_s[:, :], AF.Exp,
                                            scale=-1.0)],
             ["bc_s"], ["recip_bc"])

    def tail_D(j):
        P.op("dve", [lambda e, j=j: e.tensor_tensor(
            qctx[:, j * R:(j + 1) * R], qctx[:, j * R:(j + 1) * R],
            recip_bc[:, :], ALU.mult)],
            ["recip_bc"], [f"qctx{j}"])

    for h in range(KVH):
        par = h % 2
        if h + 1 < KVH:
            gathers_V(h + 1)
            gathers_K(h + 1, (0, 1))
        for jl in range(4):
            j = h * 4 + jl
            jprev = j - 1
            if h + 1 < KVH and jl == 1:
                transposes_K(h + 1, (0, 1))
            if h + 1 < KVH and jl == 2:
                gathers_K(h + 1, (2, 3))
                transposes_K(h + 1, (2, 3))
            for t in range(66):
                if t < 64:
                    tc = t
                    spar = tc % 2
                    se = tc % 6
                    mm([
                        lambda e, par=par, spar=spar, tc=tc, j=j: e.matmul(
                            p_sc[spar][:, 0:512], KT[par][:, tc * 128:(tc + 1) * 128],
                            qctx[:, j * R: j * R + 512], start=True, stop=True),
                        lambda e, par=par, spar=spar, tc=tc, j=j: e.matmul(
                            p_sc[spar][:, 512:1024], KT[par][:, tc * 128:(tc + 1) * 128],
                            qctx[:, j * R + 512: (j + 1) * R], start=True, stop=True),
                    ], [f"KT{par}_{tc // 16}", f"qctx{j}"], [f"sc{spar}"],
                       tokens=(p1_dve,) if (h == 0 and jl == 0 and tc <= 1) else ())
                    if t == 8 and jprev >= 0:
                        tail_B(jprev)
                    if t == 12 and jprev >= 0:
                        tail_D(jprev)
                    P.op("act", [lambda e, spar=spar, se=se: e.activation(
                        exp_bf[se][:, :], p_sc[spar][:, :], AF.Exp, scale=ISQ)],
                        [f"sc{spar}"], [f"exp{se}"])
                    if t == 10 and jprev >= 0:
                        tail_C(jprev)
                    if tc % 2 == 1:
                        e0, e1 = (tc - 1) % 6, tc % 6
                        ap = j % 2
                        if tc == 1:
                            P.op("dve", [lambda e, e0=e0, e1=e1, ap=ap: e.tensor_add(
                                acc_f2[ap][:, :], exp_bf[e0][:, :], exp_bf[e1][:, :])],
                                [f"exp{e0}", f"exp{e1}"], [f"acc{ap}"])
                        else:
                            P.op("dve", [lambda e, e0=e0, e1=e1: e.tensor_add(
                                pair_bf[:, :], exp_bf[e0][:, :], exp_bf[e1][:, :])],
                                [f"exp{e0}", f"exp{e1}"], ["pair"])
                            P.op("dve", [lambda e, ap=ap: e.tensor_add(
                                acc_f2[ap][:, :], acc_f2[ap][:, :], pair_bf[:, :])],
                                ["pair", f"acc{ap}"], [f"acc{ap}"])
                if t >= 2:
                    tc = t - 2
                    se = tc % 6
                    cp = j % 2
                    mm([
                        lambda e, cp=cp, par=par, se=se, tc=tc: e.matmul(
                            p_ctx[cp][:, 0:512], V_bf[par][:, tc * 128:(tc + 1) * 128],
                            exp_bf[se][:, 0:512], start=(tc == 0), stop=(tc == 63)),
                        lambda e, cp=cp, par=par, se=se, tc=tc: e.matmul(
                            p_ctx[cp][:, 512:1024], V_bf[par][:, tc * 128:(tc + 1) * 128],
                            exp_bf[se][:, 512:1024], start=(tc == 0), stop=(tc == 63)),
                    ], [f"V{par}_{tc // 16}", f"exp{se}"], [f"p_ctx{cp}"],
                       tokens=(p1_dve,) if (h == 0 and jl == 0 and tc == 0) else ())
            tail_A(j)
        if h == 6:
            P.w["_h6pe"] = P.token("pe")
            P.w["_h6dve"] = P.token("dve")
            P.w["_h6act"] = P.token("act")
    tail_B(NJ - 1)
    tail_C(NJ - 1)
    tail_D(NJ - 1)

    p2_pe = P.token("pe")
    p2_act = P.token("act")
    p2_dve = P.token("dve")

    p2ps.close()
    p3ps = ExitStack()
    p_S = [p3ps.enter_context(nc.psum_tensor(f"p_S{i}", [128, R], F32)) for i in range(2)]
    p_tr3 = [p3ps.enter_context(nc.psum_tensor(f"p_tr3{i}", [128, 512], BF16)) for i in range(2)]
    p_o = [p3ps.enter_context(nc.psum_tensor(f"p_o{i}", [128, 128], F32)) for i in range(2)]

    p2stk.close()
    p3stk = ExitStack()
    sb3 = lambda n, s, d: p3stk.enter_context(nc.sbuf_tensor(n, s, d))
    xnT2 = sb3("xnT2", [128, DC * R], BF16)
    wo_t = [sb3(f"wo_t{i}", [128, D], BF16) for i in range(2)]
    wf_nat = [sb3(f"wfn{i}", [128, D], BF16) for i in range(2)]
    wfT = [sb3(f"wfT{i}", [128, D], BF16) for i in range(2)]
    BT_s = sb3("BT_s", [64, D], BF16)
    b_raw = [sb3(f"b_raw{i}", [128, 64], BF16) for i in range(2)]
    sT = [sb3(f"sT{i}", [128, 512], F32) for i in range(2)]
    xp = [sb3(f"xp{i}", [128, 128], F32) for i in range(4)]
    opc = [sb3(f"opc{i}", [128, 128], F32) for i in range(4)]

    guard3 = [p2_pe, p2_act, p2_dve]

    h6toks = [t for t in (P.w.get("_h6pe"), P.w.get("_h6dve"), P.w.get("_h6act")) if t]
    for piece, toks in ((0, h6toks), (2, h6toks), (1, guard3), (3, guard3)):
        dma("sy", xnT2[:, piece * 8192:(piece + 1) * 8192],
            spill_h.ap()[:, piece * 8192:(piece + 1) * 8192],
            ["spill"], ["xnT2"], family="xnT2", tokens=toks)

    for c in range(DC):
        par = c % 2
        dma("gp", b_raw[par][:, :], bm_h.ap()[c * 128:(c + 1) * 128, :],
            [], [f"b_raw{par}"], tokens=guard3 if c <= 1 else ())
        mm([lambda e, par=par: e.transpose(
            p_tr3[par][0:64, 0:128], b_raw[par][:, :], idb[:, :])],
           [f"b_raw{par}", "idb"], [f"p_tr3{par}"])
        P.op("dve", [lambda e, c=c, par=par: e.tensor_copy(
            BT_s[:, c * 128:(c + 1) * 128], p_tr3[par][0:64, 0:128])],
            [f"p_tr3{par}"], ["BT_s"])

    pend = None
    for oc in range(DC):
        par = oc % 2
        dma("gp", wo_t[par][:, :], wo_h.ap()[oc * 128:(oc + 1) * 128, :],
            [], [f"wo{par}"], tokens=guard3 if oc <= 1 else ())
        dma("gp", wf_nat[par][:, :], wf_h.ap()[oc * 128:(oc + 1) * 128, :],
            [], [f"wfn{par}"], tokens=guard3 if oc <= 1 else ())
        for g in range(8):
            tp = g % 2
            mm([lambda e, par=par, g=g, i=i, tp=tp: e.transpose(
                p_tr3[tp][:, i * 128:(i + 1) * 128],
                wf_nat[par][:, (g * 4 + i) * 128:(g * 4 + i + 1) * 128],
                idb[:, :]) for i in range(4)],
               [f"wfn{par}", "idb"], [f"p_tr3{tp}"])
            if g % 2 == 0:
                P.op("dve", [lambda e, par=par, g=g, tp=tp: e.tensor_copy(
                    wfT[par][:, g * 512:(g + 1) * 512], p_tr3[tp][:, :])],
                    [f"p_tr3{tp}"], [f"wfT{par}"])
            else:
                P.op("act", [lambda e, par=par, g=g, tp=tp: e.copy(
                    wfT[par][:, g * 512:(g + 1) * 512], p_tr3[tp][:, :])],
                    [f"p_tr3{tp}"], [f"wfT{par}"])
        for half in range(2):
            sl = slice(half * 512, (half + 1) * 512)
            if pend is not None:
                poc, phalf = pend
                psp = phalf
                P.op("dve", [lambda e, poc=poc, phalf=phalf, psp=psp: e.tensor_scalar(
                    sT[psp][:, :], p_S[poc % 2][:, phalf * 512:(phalf + 1) * 512],
                    bT[:, poc:poc + 1], None, ALU.add)],
                    [f"p_S{poc % 2}h{phalf}", "bT"], [f"sT{psp}"])
                for q in range(4):
                    xb = (2 * phalf + q) % 4
                    dma("gp", xp[xb][:, :],
                        x_h.ap()[phalf * 512 + q * 128: phalf * 512 + (q + 1) * 128,
                                 poc * 128:(poc + 1) * 128],
                        [], [f"xp{xb}"])

            def outpiece(q):
                poc, phalf = pend
                psp = phalf
                tp = q % 2
                rv = phalf * 512 + q * 128
                xb = (2 * phalf + q) % 4
                mm([lambda e, psp=psp, q=q, tp=tp: e.transpose(
                    p_o[tp][:, :], sT[psp][:, q * 128:(q + 1) * 128], idf[:, :])],
                   [f"sT{psp}", "idf"], [f"p_o{tp}"])
                P.op("dve", [lambda e, tp=tp, xb=xb: e.tensor_add(
                    opc[xb][:, :], p_o[tp][:, :], xp[xb][:, :])],
                    [f"p_o{tp}", f"xp{xb}"], [f"opc{xb}"])
                P.op("sy", [lambda e, xb=xb, rv=rv, poc=poc: e.dma_start(
                    out=out_h.ap()[rv:rv + 128, poc * 128:(poc + 1) * 128],
                    in_=opc[xb][:, :])],
                    reads=[f"opc{xb}"], writes=[], ctr=fam_ctr(f"out{xb}"))

            for hc in range(DC):
                mm([lambda e, par=par, hc=hc, sl=sl, half=half: e.matmul(
                    p_S[par][:, sl], wo_t[par][:, hc * 128:(hc + 1) * 128],
                    qctx[:, hc * R + half * 512: hc * R + (half + 1) * 512],
                    start=(hc == 0), stop=False)],
                   [f"wo{par}", f"qctx{hc}"], [f"p_S{par}h{half}"], inc=False,
                   tokens=tuple(guard3) if (oc <= 1 and hc == 0) else ())
                if pend is not None and hc in (7, 14, 21, 28):
                    outpiece(hc // 7 - 1)
            for dc in range(DC):
                mm([lambda e, par=par, dc=dc, sl=sl, half=half: e.matmul(
                    p_S[par][:, sl], wfT[par][:, dc * 128:(dc + 1) * 128],
                    xnT2[:, dc * R + half * 512: dc * R + (half + 1) * 512],
                    start=False, stop=False)],
                   [f"wfT{par}", "xnT2"], [f"p_S{par}h{half}"], inc=False)
            mm([lambda e, par=par, oc=oc, sl=sl, half=half: e.matmul(
                p_S[par][:, sl], BT_s[:, oc * 128:(oc + 1) * 128],
                u_m[:, half * 512:(half + 1) * 512], start=False, stop=True)],
               ["BT_s", "u_m"], [f"p_S{par}h{half}"])
            pend = (oc, half)

    poc, phalf = pend
    psp = phalf
    P.op("dve", [lambda e: e.tensor_scalar(
        sT[psp][:, :], p_S[poc % 2][:, phalf * 512:(phalf + 1) * 512],
        bT[:, poc:poc + 1], None, ALU.add)],
        [f"p_S{poc % 2}h{phalf}", "bT"], [f"sT{psp}"])
    for q in range(4):
        tp = q % 2
        rv = phalf * 512 + q * 128
        xb = (2 * phalf + q) % 4
        dma("gp", xp[xb][:, :],
            x_h.ap()[rv:rv + 128, poc * 128:(poc + 1) * 128], [], [f"xp{xb}"])
        mm([lambda e, q=q, tp=tp: e.transpose(
            p_o[tp][:, :], sT[psp][:, q * 128:(q + 1) * 128], idf[:, :])],
           [f"sT{psp}", "idf"], [f"p_o{tp}"])
        P.op("dve", [lambda e, tp=tp, xb=xb: e.tensor_add(
            opc[xb][:, :], p_o[tp][:, :], xp[xb][:, :])],
            [f"p_o{tp}", f"xp{xb}"], [f"opc{xb}"])
        P.op("sy", [lambda e, xb=xb, rv=rv: e.dma_start(
            out=out_h.ap()[rv:rv + 128, poc * 128:(poc + 1) * 128],
            in_=opc[xb][:, :])],
            reads=[f"opc{xb}"], writes=[], ctr=fam_ctr(f"out{xb}"))

    for xb in range(4):
        key = "f_" + f"out{xb}"
        semh, val, _ = P.ctr[key]
        P.lists["sy"].append(lambda e, s=semh, v=val: e.wait_ge(s, v))

    with nc.Block() as block:
        @block.tensor
        def _(t):
            for f in P.lists["pe"]:
                f(t)

        @block.scalar
        def _(s):
            for f in P.lists["act"]:
                f(s)

        @block.vector
        def _(v):
            for f in P.lists["dve"]:
                f(v)

        @block.gpsimd
        def _(g):
            for f in P.lists["gp"]:
                f(g)

        @block.sync
        def _(sy):
            for f in P.lists["sy"]:
                f(sy)

    p3ps.close()
    p3stk.close()
    gstk.close()
    return nc


def kernel(L0_attn_hidden_states, L0_attn_Wq, L0_attn_Wo, L0_attn_kv_cache,
           L0_attn_block_table, L0_ffn_W, L0_ffn_b, L0_ffn_A, L0_ffn_B,
           L0_ffn_gate_w, L0_ffn_gate_b):
    x = np.ascontiguousarray(np.asarray(L0_attn_hidden_states, dtype=np.float32)).reshape(8192, 4096)
    slab = lambda W: np.ascontiguousarray(
        W.reshape(32, 128, 32, 128).transpose(2, 1, 0, 3).reshape(4096, 4096))
    wq = slab(np.asarray(L0_attn_Wq, dtype=np.float32))
    wo = slab(np.asarray(L0_attn_Wo, dtype=np.float32))
    wf = np.ascontiguousarray(np.asarray(L0_ffn_W, dtype=np.float32))
    kvc = np.ascontiguousarray(np.asarray(L0_attn_kv_cache, dtype=np.float32))
    btb = np.ascontiguousarray(np.asarray(L0_attn_block_table, dtype=np.int32).reshape(512, 1))
    am = np.ascontiguousarray(np.asarray(L0_ffn_A, dtype=np.float32)
                              .reshape(64, 32, 128).transpose(2, 1, 0).reshape(128, 2048))
    bm = np.ascontiguousarray(np.asarray(L0_ffn_B, dtype=np.float32))
    gw = np.ascontiguousarray(np.asarray(L0_ffn_gate_w, dtype=np.float32)
                              .reshape(4, 32, 128).transpose(2, 1, 0).reshape(128, 128))
    gb = np.ascontiguousarray(np.asarray(L0_ffn_gate_b, dtype=np.float32).reshape(1, 4))
    bv = np.ascontiguousarray(np.asarray(L0_ffn_b, dtype=np.float32).reshape(32, 128).T)

    idf = np.eye(128, dtype=np.float32)
    idb = np.eye(128, dtype=np.float32).astype(ml_dtypes.bfloat16)
    iota = np.arange(64, dtype=np.float32).reshape(64, 1)
    inv = 1.0 / 10000.0 ** (np.arange(0, 128, 2, dtype=np.float32) / 128.0)
    ang = np.arange(4096, dtype=np.float32)[:, None] * inv[None, :]
    emb = np.concatenate([ang, ang], axis=1)
    cos_t = np.ascontiguousarray(np.cos(emb).astype(np.float32).T)
    sin_t = np.sin(emb).astype(np.float32)
    sin_t[:, :64] *= -1.0
    sin_t = np.ascontiguousarray(sin_t.T)

    nc = build()
    in_maps = []
    for c in range(8):
        s0 = (c * R) % 4096
        in_maps.append({
            "x": np.ascontiguousarray(x[c * R:(c + 1) * R]),
            "wq": wq, "wo": wo, "wf": wf, "kvc": kvc, "btb": btb,
            "am": am, "bm": bm, "gw": gw, "gbrow": gb, "bvec": bv,
            "cosT": np.ascontiguousarray(cos_t[:, s0:s0 + R]),
            "sinT": np.ascontiguousarray(sin_t[:, s0:s0 + R]),
            "idf": idf, "idb": idb, "iotac": iota,
        })
    res = run_bass_kernel_spmd(nc, in_maps, core_ids=list(range(8)))
    out = np.concatenate([np.asarray(res.results[c]["out"]) for c in range(8)], axis=0)
    return out.reshape(2, 4096, 4096)


# version 28 (final)
# speedup vs baseline: 1.1308x; 1.0638x over previous
import math
from contextlib import ExitStack

import numpy as np
import ml_dtypes

import concourse.bass as bass
import concourse.mybir as mybir
from concourse.bass_utils import run_bass_kernel_spmd
from concourse.bass_core_rust import IndirectOffsetOnAxis

F32 = mybir.dt.float32
BF16 = mybir.dt.bfloat16
I32 = mybir.dt.int32
AF = mybir.ActivationFunctionType
ALU = mybir.AluOpType

R = 1024
D = 4096
DC = 32
NJ = 32
KVH = 8
EPS = 1e-5
ISQ = 1.0 / math.sqrt(128.0)

ENGS = ("pe", "act", "dve", "gp", "sy", "sc")
DMA_ENGS = ("gp", "sy", "sc")


class Prog:

    def __init__(self):
        self.lists = {e: [] for e in ENGS}
        self.lists["sc"] = self.lists["act"]
        self.idx = {e: 0 for e in ENGS}
        self.ctr = {}
        self.eng_ctr = {}
        self.hiwater = {e: {} for e in ENGS}
        self.w = {}
        self.r = {}
        self.pending = {e: [] for e in ENGS}

    def add_ctr(self, name, sem, step):
        self.ctr[name] = [sem, 0, step]

    def token(self, eng, cname=None):
        cname = cname or self.eng_ctr[eng]
        sem, n, _ = self.ctr[cname]
        return (eng, cname, n, self.idx[eng])

    def _emit_wait(self, eng, tok):
        teng, cname, val, tidx = tok
        if val <= 0:
            return
        if teng == eng and teng not in DMA_ENGS:
            if self.idx[eng] - tidx > 6:
                return
        sem = self.ctr[cname][0]
        hw = self.hiwater[eng]
        if hw.get(cname, -1) >= val:
            return
        hw[cname] = val
        self.lists[eng].append(lambda e, s=sem, v=val: e.wait_ge(s, v))
        self.idx[eng] += 1

    def op(self, eng, calls, reads=(), writes=(), inc=True, ctr=None, tokens=()):
        toks = list(tokens)
        for res in reads:
            t = self.w.get(res)
            if t is not None:
                toks.append(t)
        for res in writes:
            t = self.w.get(res)
            if t is not None:
                toks.append(t)
            toks.extend(self.r.get(res, ()))
        best = {}
        for t in toks:
            if t is None:
                continue
            k = (t[0], t[1])
            if k not in best or t[2] > best[k][2]:
                best[k] = t
        for t in best.values():
            self._emit_wait(eng, t)
        cname = ctr or self.eng_ctr[eng]
        for i, call in enumerate(calls):
            if i == len(calls) - 1 and inc:
                c = self.ctr[cname]
                c[1] += c[2]
                self.lists[eng].append(
                    lambda e, f=call, s=c[0], st=c[2]: f(e).then_inc(s, st))
            else:
                self.lists[eng].append(lambda e, f=call: f(e))
            self.idx[eng] += 1
        if inc:
            tok = self.token(eng, cname)
            for kind, res in self.pending[eng]:
                self.r.setdefault(res, []).append(tok)
            self.pending[eng] = []
            for res in writes:
                self.w[res] = tok
                self.r[res] = []
            for res in reads:
                self.r.setdefault(res, []).append(tok)
            return tok
        else:
            for res in writes:
                self.pending[eng].append(("w", res))
            for res in reads:
                self.pending[eng].append(("r", res))
            return None


def build():
    nc = bass.Bass("TRN2", target_bir_lowering=False)

    x_h = nc.dram_tensor("x", [R, D], F32, kind="ExternalInput")
    wq_h = nc.dram_tensor("wq", [D, D], F32, kind="ExternalInput")
    wo_h = nc.dram_tensor("wo", [D, D], F32, kind="ExternalInput")
    wf_h = nc.dram_tensor("wf", [D, D], F32, kind="ExternalInput")
    kvc_h = nc.dram_tensor("kvc", [1024, 2, 8, 16, 128], F32, kind="ExternalInput")
    btb_h = nc.dram_tensor("btb", [512, 1], I32, kind="ExternalInput")
    am_h = nc.dram_tensor("am", [128, DC * 64], F32, kind="ExternalInput")
    bm_h = nc.dram_tensor("bm", [64, D], F32, kind="ExternalInput")
    gw_h = nc.dram_tensor("gw", [128, 128], F32, kind="ExternalInput")
    gb_h = nc.dram_tensor("gbrow", [1, 4], F32, kind="ExternalInput")
    bv_h = nc.dram_tensor("bvec", [128, 32], F32, kind="ExternalInput")
    cos_h = nc.dram_tensor("cosT", [128, R], F32, kind="ExternalInput")
    sin_h = nc.dram_tensor("sinT", [128, R], F32, kind="ExternalInput")
    idf_h = nc.dram_tensor("idf", [128, 128], F32, kind="ExternalInput")
    idb_h = nc.dram_tensor("idb", [128, 128], BF16, kind="ExternalInput")
    iota_h = nc.dram_tensor("iotac", [64, 1], F32, kind="ExternalInput")
    out_h = nc.dram_tensor("out", [R, D], F32, kind="ExternalOutput")
    spill_h = nc.dram_tensor("xnspill", [128, DC * R], BF16, kind="Internal")

    kv_flat = kvc_h.ap().rearrange("b s h t d -> b (s h t d)")

    P = Prog()
    gstk = ExitStack()
    sb = lambda n, s, d: gstk.enter_context(nc.sbuf_tensor(n, s, d))
    sem = lambda n: gstk.enter_context(nc.semaphore(n))

    s_pe = sem("s_pe"); s_act = sem("s_act"); s_dve = sem("s_dve")
    s_gp = sem("s_gp"); s_sy = sem("s_sy"); s_sq = sem("s_sq"); s_out = sem("s_out")
    for nm, smh, st in (("pe", s_pe, 1), ("act", s_act, 1), ("dve", s_dve, 1),
                        ("gp", s_gp, 16), ("sy", s_sy, 16), ("sc", s_sq, 16),
                        ("out", s_out, 16)):
        P.add_ctr(nm, smh, st)
    P.eng_ctr = {"pe": "pe", "act": "act", "dve": "dve", "gp": "gp",
                 "sy": "sy", "sc": "sc"}

    qctx = sb("qctx", [128, NJ * R], BF16)
    idf = sb("idf_s", [128, 128], F32)
    idb = sb("idb_s", [128, 128], BF16)
    ones_r = sb("ones_r", [1, 128], F32)
    ones_cf = sb("ones_cf", [128, 1], F32)
    ones_sq = sb("ones_sq", [128, 128], F32)
    iota_s = sb("iota_s", [64, 1], F32)
    bT = sb("bT", [128, 32], F32)
    btcols = sb("btcols", [128, 4], I32)
    u_m = sb("u_m", [64, R], BF16)
    gsc = sb("gsc", [128, 4], F32)
    gbrow = sb("gbrow_s", [1, 4], F32)
    stats = {n: sb(n, [128, 1], F32) for n in
             ("sumA", "sumB", "sqA", "sqB", "sums", "sumsq", "mean", "msq",
              "var", "t1", "rstd", "negmr", "ge01", "m01", "t01", "ge23",
              "m23", "t23", "gef", "dd", "thrc", "epsc")}

    def fam_ctr(name):
        key = "f_" + name
        if key not in P.ctr:
            P.add_ctr(key, gstk.enter_context(nc.semaphore(key)), 16)
        return key

    def dma(eng, out_ap, in_ap, reads, writes, family=None, tokens=()):
        fam = fam_ctr(family or writes[0])
        return P.op(eng, [lambda e, o=out_ap, i=in_ap: e.dma_start(out=o, in_=i)],
                    reads=reads, writes=writes, ctr=fam, tokens=tokens)

    def mm(calls, reads, writes, inc=True, tokens=()):
        return P.op("pe", calls, reads=reads, writes=writes, inc=inc, tokens=tokens)

    p0stk = ExitStack()
    sb0 = lambda n, s, d: p0stk.enter_context(nc.sbuf_tensor(n, s, d))
    xr = [sb0(f"xr{i}", [128, 2048], F32) for i in range(2)]
    xnf = [sb0(f"xnf{i}", [128, 2048], BF16) for i in range(2)]
    xnT = sb0("xnT", [128, DC * R], BF16)
    wq_t = [sb0(f"wq_t{i}", [128, D], BF16) for i in range(2)]
    AT = sb0("AT", [128, DC * 64], BF16)
    gwT = sb0("gwT", [128, 128], BF16)

    thr_row = sb0("thr_row", [1, R], F32)
    mask_bf = sb0("mask_bf", [64, R], BF16)

    p0ps = ExitStack()
    ps0 = lambda n, s, d: p0ps.enter_context(nc.psum_tensor(n, s, d))
    p_g = ps0("p_g", [128, 4], F32)
    p_thr = ps0("p_thr", [1, 128], F32)
    p_trx = [ps0(f"p_trx{i}", [128, 256], BF16) for i in range(2)]

    dma("sy", idf[:, :], idf_h.ap(), [], ["idf"])
    dma("sy", idb[:, :], idb_h.ap(), [], ["idb"])
    dma("sy", iota_s[:, :], iota_h.ap(), [], ["iota"])
    dma("sy", bT[:, :], bv_h.ap(), [], ["bT"])
    dma("sy", gbrow[:, :], gb_h.ap(), [], ["gbrow"])
    for i in range(4):
        dma("sy", btcols[:, i:i + 1], btb_h.ap()[i * 128:(i + 1) * 128, 0:1],
            [], [f"btc{i}"])
    P.op("dve", [lambda e: e.memset(ones_r[:, :], 1.0)], [], ["ones_r"])
    P.op("dve", [lambda e: e.memset(stats["epsc"][:, :], EPS)], [], ["epsc"])
    P.op("dve", [lambda e: e.memset(ones_cf[:, :], 1.0)], [], ["ones_cf"])
    P.op("dve", [lambda e: e.memset(ones_sq[:, :], 1.0)], [], ["ones_sq"])

    dma("gp", gwT[:, :], gw_h.ap(), [], ["gwT"])
    dma("gp", AT[:, :], am_h.ap(), [], ["AT"])

    for r in range(8):
        for hf in range(2):
            par = (2 * r + hf) % 2
            dma("sy", xr[par][:, :],
                x_h.ap()[r * 128:(r + 1) * 128, hf * 2048:(hf + 1) * 2048],
                [], [f"xr{par}"])
            sq = stats["sqA" if hf == 0 else "sqB"]
            sm = stats["sumA" if hf == 0 else "sumB"]
            P.op("act", [lambda e, par=par, sq=sq: e.activation(
                xnf[par][:, :], xr[par][:, :], AF.Square, accum_out=sq[:, :])],
                [f"xr{par}"], [f"xnf{par}", f"sq{hf}"])
            P.op("act", [lambda e, par=par, sm=sm: e.activation(
                xnf[par][:, :], xr[par][:, :], AF.Identity, accum_out=sm[:, :])],
                [f"xr{par}"], [f"xnf{par}", f"sum{hf}"])
        st = stats
        P.op("dve", [lambda e: e.tensor_add(st["sums"][:, :], st["sumA"][:, :], st["sumB"][:, :])],
             ["sum0", "sum1"], ["sums"])
        P.op("dve", [lambda e: e.tensor_add(st["sumsq"][:, :], st["sqA"][:, :], st["sqB"][:, :])],
             ["sq0", "sq1"], ["sumsq"])
        P.op("dve", [lambda e: e.tensor_scalar_mul(st["mean"][:, :], st["sums"][:, :], 1.0 / D)],
             ["sums"], ["mean"])
        P.op("dve", [lambda e: e.tensor_mul(st["msq"][:, :], st["mean"][:, :], st["mean"][:, :])],
             ["mean"], ["msq"])
        P.op("dve", [lambda e: e.tensor_scalar_mul(st["var"][:, :], st["sumsq"][:, :], 1.0 / D)],
             ["sumsq"], ["var"])
        P.op("dve", [lambda e: e.tensor_sub(st["var"][:, :], st["var"][:, :], st["msq"][:, :])],
             ["msq", "var"], ["var"])
        P.op("act", [lambda e: e.activation(st["t1"][:, :], st["var"][:, :], AF.Ln, bias=st["epsc"][:, :])],
             ["var", "epsc"], ["t1"])
        P.op("act", [lambda e: e.activation(st["rstd"][:, :], st["t1"][:, :], AF.Exp, scale=-0.5)],
             ["t1"], ["rstd"])
        P.op("dve", [lambda e: e.scalar_tensor_tensor(
            st["negmr"][:, :], st["mean"][:, :], -1.0, st["rstd"][:, :],
            ALU.mult, ALU.mult)], ["mean", "rstd"], ["negmr"])
        for hf in range(2):
            par = (2 * r + hf) % 2
            P.op("act", [lambda e, par=par: e.activation(
                xnf[par][:, :], xr[par][:, :], AF.Identity,
                bias=st["negmr"][:, :], scale=st["rstd"][:, :])],
                [f"xr{par}", "negmr", "rstd"], [f"xnf{par}"])
            for k2 in range(8):
                dc0 = hf * 16 + 2 * k2
                tp = k2 % 2
                mm([
                    lambda e, par=par, k2=k2, tp=tp: e.transpose(
                        p_trx[tp][:, 0:128],
                        xnf[par][:, (2 * k2) * 128:(2 * k2 + 1) * 128], idb[:, :]),
                    lambda e, par=par, k2=k2, tp=tp: e.transpose(
                        p_trx[tp][:, 128:256],
                        xnf[par][:, (2 * k2 + 1) * 128:(2 * k2 + 2) * 128], idb[:, :]),
                ], [f"xnf{par}", "idb"], [f"p_trx{tp}"])
                P.op("dve", [
                    lambda e, dc0=dc0, r=r, tp=tp: e.tensor_copy(
                        xnT[:, dc0 * R + r * 128: dc0 * R + (r + 1) * 128],
                        p_trx[tp][:, 0:128]),
                    lambda e, dc0=dc0, r=r, tp=tp: e.tensor_copy(
                        xnT[:, (dc0 + 1) * R + r * 128: (dc0 + 1) * R + (r + 1) * 128],
                        p_trx[tp][:, 128:256]),
                ], [f"p_trx{tp}"], ["xnT"])
                mm([
                    lambda e, dc0=dc0, r=r: e.matmul(
                        p_g[:, :], xnT[:, dc0 * R + r * 128: dc0 * R + (r + 1) * 128],
                        gwT[:, dc0 * 4:(dc0 + 1) * 4],
                        start=(dc0 == 0), stop=False),
                    lambda e, dc0=dc0, r=r: e.matmul(
                        p_g[:, :], xnT[:, (dc0 + 1) * R + r * 128: (dc0 + 1) * R + (r + 1) * 128],
                        gwT[:, (dc0 + 1) * 4:(dc0 + 2) * 4],
                        start=False, stop=False),
                ], ["xnT", "gwT"], ["p_g"], inc=False)
        mm([lambda e: e.matmul(p_g[:, :], ones_r[:, :], gbrow[:, :],
                               start=False, stop=True)],
           ["ones_r", "gbrow"], ["p_g"])
        P.op("dve", [lambda e: e.tensor_copy(gsc[:, :], p_g[:, :])], ["p_g"], ["gsc"])
        s0v, s1v, s2v, s3v = (gsc[:, i:i + 1] for i in range(4))
        P.op("dve", [lambda e, a=s0v, b=s1v: e.tensor_tensor(st["ge01"][:, :], a, b, ALU.is_ge)], ["gsc"], ["ge01"])
        P.op("dve", [lambda e, a=s0v, b=s1v: e.tensor_tensor(st["m01"][:, :], a, b, ALU.max)], ["gsc"], ["m01"])
        P.op("dve", [lambda e, a=s2v, b=s3v: e.tensor_tensor(st["ge23"][:, :], a, b, ALU.is_ge)], ["gsc"], ["ge23"])
        P.op("dve", [lambda e, a=s2v, b=s3v: e.tensor_tensor(st["m23"][:, :], a, b, ALU.max)], ["gsc"], ["m23"])
        P.op("dve", [lambda e: e.tensor_scalar(st["t01"][:, :], st["ge01"][:, :], -8.0, 16.0, ALU.mult, ALU.add)],
             ["ge01"], ["t01"])
        P.op("dve", [lambda e: e.tensor_scalar(st["t23"][:, :], st["ge23"][:, :], -32.0, 64.0, ALU.mult, ALU.add)],
             ["ge23"], ["t23"])
        P.op("dve", [lambda e: e.tensor_tensor(st["gef"][:, :], st["m01"][:, :], st["m23"][:, :], ALU.is_ge)],
             ["m01", "m23"], ["gef"])
        P.op("dve", [lambda e: e.tensor_sub(st["dd"][:, :], st["t01"][:, :], st["t23"][:, :])],
             ["t01", "t23"], ["dd"])
        P.op("dve", [lambda e: e.tensor_mul(st["dd"][:, :], st["dd"][:, :], st["gef"][:, :])],
             ["dd", "gef"], ["dd"])
        P.op("dve", [lambda e: e.tensor_add(st["thrc"][:, :], st["t23"][:, :], st["dd"][:, :])],
             ["t23", "dd"], ["thrc"])
        mm([lambda e: e.transpose(p_thr[:, :], st["thrc"][:, :], idf[:, :])],
           ["thrc", "idf"], ["p_thr"])
        P.op("dve", [lambda e, r=r: e.tensor_copy(thr_row[0:1, r * 128:(r + 1) * 128], p_thr[:, :])],
             ["p_thr"], ["thr_row"])

    p0_act = P.token("act")
    p0_dve = P.token("dve")

    p0ps.close()
    p1aps = ExitStack()
    p_u = p1aps.enter_context(nc.psum_tensor("p_u", [64, R], F32))
    p_thrb = p1aps.enter_context(nc.psum_tensor("p_thrb", [64, R], F32))

    for dc in range(DC):
        for half in range(2):
            last = dc == DC - 1 and half == 1
            mm([lambda e, dc=dc, half=half: e.matmul(
                p_u[:, half * 512:(half + 1) * 512],
                AT[:, dc * 64:(dc + 1) * 64],
                xnT[:, dc * R + half * 512: dc * R + (half + 1) * 512],
                start=(dc == 0), stop=(dc == DC - 1))],
               ["AT", "xnT"], ["p_u"], inc=last,
               tokens=(p0_dve,) if dc == 0 else ())
    for half in range(2):
        mm([lambda e, half=half: e.matmul(
            p_thrb[:, half * 512:(half + 1) * 512], ones_r[0:1, 0:64],
            thr_row[0:1, half * 512:(half + 1) * 512], start=True, stop=True)],
           ["ones_r", "thr_row"], ["p_thrb"], inc=(half == 1), tokens=(p0_dve,))
    P.op("dve", [lambda e: e.tensor_scalar(mask_bf[:, :], p_thrb[:, :],
                                           iota_s[:, :], None, ALU.is_gt)],
         ["p_thrb", "iota"], ["mask_bf"])
    P.op("dve", [lambda e: e.tensor_tensor(u_m[:, :], p_u[:, :], mask_bf[:, :], ALU.mult)],
         ["p_u", "mask_bf"], ["u_m"])
    g_p1a = P.token("dve")

    dma("sy", spill_h.ap(), xnT[:, :], ["xnT"], ["spill"])

    cosv = xnf[0][:, :].bitcast(F32)
    sinv = xnf[1][:, :].bitcast(F32)
    dma("sy", cosv, cos_h.ap(), [], ["xnf0"])
    dma("sy", sinv, sin_h.ap(), [], ["xnf1"])
    tmp1 = xr[0][:, 0:1024]
    tmp2 = xr[1][:, 0:1024]

    p1aps.close()
    p1bps = ExitStack()
    p_q = [p1bps.enter_context(nc.psum_tensor(f"p_q{i}", [128, R], F32))
           for i in range(2)]

    for j in range(NJ):
        par = j % 2
        dma("gp", wq_t[par][:, :], wq_h.ap()[j * 128:(j + 1) * 128, :],
            [], [f"wq{par}"])
        for dc in range(DC):
            for half in range(2):
                last = dc == DC - 1 and half == 1
                mm([lambda e, par=par, dc=dc, half=half: e.matmul(
                    p_q[par][:, half * 512:(half + 1) * 512],
                    wq_t[par][:, dc * 128:(dc + 1) * 128],
                    xnT[:, dc * R + half * 512: dc * R + (half + 1) * 512],
                    start=(dc == 0), stop=(dc == DC - 1))],
                   [f"wq{par}", "xnT"], [f"p_q{par}"], inc=last,
                   tokens=(g_p1a,) if (j <= 1 and dc == 0) else ())
        P.op("dve", [lambda e, par=par: e.tensor_mul(tmp1[:, :], p_q[par][:, :], cosv)],
             [f"p_q{par}", "xnf0"], ["xr0"])
        P.op("dve", [
            lambda e, par=par: e.tensor_mul(tmp2[0:64, :], p_q[par][64:128, :], sinv[0:64, :]),
            lambda e, par=par: e.tensor_mul(tmp2[64:128, :], p_q[par][0:64, :], sinv[64:128, :]),
        ], [f"p_q{par}", "xnf1"], ["xr1"])
        P.op("dve", [lambda e, j=j: e.tensor_add(
            qctx[:, j * R:(j + 1) * R], tmp1[:, :], tmp2[:, :])],
            ["xr0", "xr1", f"p_q{par}"], [f"qctx{j}"])

    p1_pe = P.token("pe")
    p1_dve = P.token("dve")
    spill_tok = P.w["spill"]

    p1bps.close()
    p2ps = ExitStack()
    p_sc = [p2ps.enter_context(nc.psum_tensor(f"p_sc{i}", [128, R], F32)) for i in range(2)]
    p_ctx = [p2ps.enter_context(nc.psum_tensor(f"p_ctx{i}", [128, R], F32)) for i in range(2)]

    p0stk.close()
    p2stk = ExitStack()
    sb2 = lambda n, s, d: p2stk.enter_context(nc.sbuf_tensor(n, s, d))
    V_bf = [sb2(f"V{i}", [128, 8192], BF16) for i in range(2)]
    KT = [sb2(f"KT{i}", [128, 8192], BF16) for i in range(2)]
    kraw = [sb2(f"kraw{i}", [128, 2048], BF16) for i in range(2)]
    exp_bf = [sb2(f"exp{i}", [128, R], BF16) for i in range(6)]
    pair_bf = sb2("pair", [128, R], BF16)
    acc_f2 = [sb2(f"acc{i}", [128, R], F32) for i in range(2)]
    bc_s = sb2("bc_s", [128, R], F32)
    recip_bc = sb2("recip_bc", [128, R], F32)

    guard = [t for t in (p0_act, p1_pe, p1_dve, spill_tok) if t is not None]

    def gather(dst_ap, tile_i, elem_off, writes, tokens):
        P.op("gp", [lambda e, d=dst_ap, t=tile_i, eo=elem_off: e.indirect_dma_start(
            d, None, kv_flat, IndirectOffsetOnAxis(btcols[:, t:t + 1], axis=0),
            element_offset=eo)],
            reads=[f"btc{tile_i}"], writes=writes, tokens=tokens,
            ctr=fam_ctr(writes[0]))

    def gathers_V(hn):
        par = hn % 2
        toks = guard if hn <= 1 else ()
        for ti in range(4):
            gather(V_bf[par][:, ti * 2048:(ti + 1) * 2048], ti,
                   16384 + hn * 2048, [f"V{par}_{ti}"], toks)

    def gathers_K(hn, tiles):
        toks = guard if hn <= 1 else ()
        for ti in tiles:
            kp = ti % 2
            gather(kraw[kp][:, :], ti, hn * 2048, [f"kraw{kp}"], toks)

    def transposes_K(hn, tiles):
        for ti in tiles:
            par_n = hn % 2
            kp = ti % 2
            for c in range(16):
                tc_ = ti * 16 + c
                P.op("sy", [lambda e, kp=kp, c=c, par_n=par_n, tc_=tc_:
                            e.dma_start_transpose(
                                KT[par_n][:, tc_ * 128:(tc_ + 1) * 128],
                                kraw[kp][:, c * 128:(c + 1) * 128])],
                     reads=[f"kraw{kp}"], writes=[f"KT{par_n}_{ti}"],
                     ctr=fam_ctr(f"KT{par_n}_{ti}"))

    gathers_V(0)
    gathers_K(0, (0, 1))

    def transposes_K0(tiles):
        for ti in tiles:
            kp = ti % 2
            pcv = p_ctx[ti % 2][:, :].bitcast(BF16)
            for cq in range(4):
                mm([lambda e, kp=kp, cq=cq, i=i, pcv=pcv: e.transpose(
                    pcv[:, (cq * 4 + i) * 128:(cq * 4 + i + 1) * 128],
                    kraw[kp][:, (cq * 4 + i) * 128:(cq * 4 + i + 1) * 128],
                    idb[:, :]) for i in range(4)],
                   [f"kraw{kp}", "idb"], [f"p_ctx{ti % 2}"],
                   tokens=(p1_dve,) if ti < 2 and cq == 0 else ())
                P.op("dve", [lambda e, ti=ti, cq=cq, pcv=pcv: e.tensor_copy(
                    KT[0][:, (ti * 16 + cq * 4) * 128:(ti * 16 + cq * 4 + 4) * 128],
                    pcv[:, (cq * 4) * 128:(cq * 4 + 4) * 128])],
                    [f"p_ctx{ti % 2}"], [f"KT0_{ti}"])

    transposes_K0((0, 1))
    gathers_K(0, (2, 3))
    transposes_K0((2, 3))

    def tail_A(j):
        cp = j % 2
        P.op("dve", [lambda e, j=j, cp=cp: e.tensor_copy(
            qctx[:, j * R:(j + 1) * R], p_ctx[cp][:, :])],
            [f"p_ctx{cp}"], [f"qctx{j}"])

    def tail_B(j):
        ap = j % 2
        mm([
            lambda e, ap=ap: e.matmul(p_ctx[ap][:, 0:512], ones_sq[:, :],
                               acc_f2[ap][:, 0:512], start=True, stop=True),
            lambda e, ap=ap: e.matmul(p_ctx[ap][:, 512:1024], ones_sq[:, :],
                               acc_f2[ap][:, 512:1024], start=True, stop=True),
        ], [f"acc{ap}", "ones_sq"], [f"p_ctx{ap}"])

    def tail_C(j):
        cp = j % 2
        P.op("act", [lambda e, cp=cp: e.activation(bc_s[:, :], p_ctx[cp][:, :], AF.Ln)],
             [f"p_ctx{cp}"], ["bc_s"])
        P.op("act", [lambda e: e.activation(recip_bc[:, :], bc_s[:, :], AF.Exp,
                                            scale=-1.0)],
             ["bc_s"], ["recip_bc"])

    def tail_D(j):
        P.op("dve", [lambda e, j=j: e.tensor_tensor(
            qctx[:, j * R:(j + 1) * R], qctx[:, j * R:(j + 1) * R],
            recip_bc[:, :], ALU.mult)],
            ["recip_bc"], [f"qctx{j}"])

    for h in range(KVH):
        par = h % 2
        if h + 1 < KVH:
            gathers_V(h + 1)
            gathers_K(h + 1, (0, 1))
        for jl in range(4):
            j = h * 4 + jl
            jprev = j - 1
            if h + 1 < KVH and jl == 1:
                transposes_K(h + 1, (0, 1))
            if h + 1 < KVH and jl == 2:
                gathers_K(h + 1, (2, 3))
                transposes_K(h + 1, (2, 3))
            for t in range(66):
                if t < 64:
                    tc = t
                    spar = tc % 2
                    se = tc % 6
                    mm([
                        lambda e, par=par, spar=spar, tc=tc, j=j: e.matmul(
                            p_sc[spar][:, 0:512], KT[par][:, tc * 128:(tc + 1) * 128],
                            qctx[:, j * R: j * R + 512], start=True, stop=True),
                        lambda e, par=par, spar=spar, tc=tc, j=j: e.matmul(
                            p_sc[spar][:, 512:1024], KT[par][:, tc * 128:(tc + 1) * 128],
                            qctx[:, j * R + 512: (j + 1) * R], start=True, stop=True),
                    ], [f"KT{par}_{tc // 16}", f"qctx{j}"], [f"sc{spar}"],
                       tokens=(p1_dve,) if (h == 0 and jl == 0 and tc <= 1) else ())
                    if t == 8 and jprev >= 0:
                        tail_B(jprev)
                    if t == 12 and jprev >= 0:
                        tail_D(jprev)
                    P.op("act", [lambda e, spar=spar, se=se: e.activation(
                        exp_bf[se][:, :], p_sc[spar][:, :], AF.Exp, scale=ISQ)],
                        [f"sc{spar}"], [f"exp{se}"])
                    if t == 10 and jprev >= 0:
                        tail_C(jprev)
                    if tc % 2 == 1:
                        e0, e1 = (tc - 1) % 6, tc % 6
                        ap = j % 2
                        if tc == 1:
                            P.op("dve", [lambda e, e0=e0, e1=e1, ap=ap: e.tensor_add(
                                acc_f2[ap][:, :], exp_bf[e0][:, :], exp_bf[e1][:, :])],
                                [f"exp{e0}", f"exp{e1}"], [f"acc{ap}"])
                        else:
                            P.op("dve", [lambda e, e0=e0, e1=e1: e.tensor_add(
                                pair_bf[:, :], exp_bf[e0][:, :], exp_bf[e1][:, :])],
                                [f"exp{e0}", f"exp{e1}"], ["pair"])
                            P.op("dve", [lambda e, ap=ap: e.tensor_add(
                                acc_f2[ap][:, :], acc_f2[ap][:, :], pair_bf[:, :])],
                                ["pair", f"acc{ap}"], [f"acc{ap}"])
                if t >= 2:
                    tc = t - 2
                    se = tc % 6
                    cp = j % 2
                    mm([
                        lambda e, cp=cp, par=par, se=se, tc=tc: e.matmul(
                            p_ctx[cp][:, 0:512], V_bf[par][:, tc * 128:(tc + 1) * 128],
                            exp_bf[se][:, 0:512], start=(tc == 0), stop=(tc == 63)),
                        lambda e, cp=cp, par=par, se=se, tc=tc: e.matmul(
                            p_ctx[cp][:, 512:1024], V_bf[par][:, tc * 128:(tc + 1) * 128],
                            exp_bf[se][:, 512:1024], start=(tc == 0), stop=(tc == 63)),
                    ], [f"V{par}_{tc // 16}", f"exp{se}"], [f"p_ctx{cp}"],
                       tokens=(p1_dve,) if (h == 0 and jl == 0 and tc == 0) else ())
            tail_A(j)
        if h == 6:
            P.w["_h6pe"] = P.token("pe")
            P.w["_h6dve"] = P.token("dve")
            P.w["_h6act"] = P.token("act")
    tail_B(NJ - 1)
    tail_C(NJ - 1)
    tail_D(NJ - 1)

    p2_pe = P.token("pe")
    p2_act = P.token("act")
    p2_dve = P.token("dve")

    p2ps.close()
    p3ps = ExitStack()
    p_S = [p3ps.enter_context(nc.psum_tensor(f"p_S{i}", [128, R], F32)) for i in range(2)]
    p_o = [p3ps.enter_context(nc.psum_tensor(f"p_o{i}", [128, 128], F32)) for i in range(2)]

    p2stk.close()
    p3stk = ExitStack()
    sb3 = lambda n, s, d: p3stk.enter_context(nc.sbuf_tensor(n, s, d))
    xnT2 = sb3("xnT2", [128, DC * R], BF16)
    wo_t = [sb3(f"wo_t{i}", [128, D], BF16) for i in range(2)]
    wfT = [sb3(f"wfT{i}", [128, D], BF16) for i in range(2)]
    BT_s = sb3("BT_s", [64, D], BF16)
    sT = [sb3(f"sT{i}", [128, 512], F32) for i in range(2)]
    xp = [sb3(f"xp{i}", [128, 128], F32) for i in range(4)]
    opc = [sb3(f"opc{i}", [128, 128], F32) for i in range(4)]

    guard3 = [p2_pe, p2_act, p2_dve]

    h6toks = [t for t in (P.w.get("_h6pe"), P.w.get("_h6dve"), P.w.get("_h6act")) if t]
    for piece, toks in ((0, h6toks), (2, h6toks), (1, guard3), (3, guard3)):
        dma("sy", xnT2[:, piece * 8192:(piece + 1) * 8192],
            spill_h.ap()[:, piece * 8192:(piece + 1) * 8192],
            ["spill"], ["xnT2"], family="xnT2", tokens=toks)

    dma("gp", BT_s[:, :], bm_h.ap(), [], ["BT_s"], tokens=guard3)

    pend = None
    for oc in range(DC):
        par = oc % 2
        dma("gp", wo_t[par][:, :], wo_h.ap()[oc * 128:(oc + 1) * 128, :],
            [], [f"wo{par}"], tokens=guard3 if oc <= 1 else ())
        dma("gp", wfT[par][:, :], wf_h.ap()[oc * 128:(oc + 1) * 128, :],
            [], [f"wfT{par}"], tokens=guard3 if oc <= 1 else ())
        for half in range(2):
            sl = slice(half * 512, (half + 1) * 512)
            if pend is not None:
                poc, phalf = pend
                psp = phalf
                P.op("dve", [lambda e, poc=poc, phalf=phalf, psp=psp: e.tensor_scalar(
                    sT[psp][:, :], p_S[poc % 2][:, phalf * 512:(phalf + 1) * 512],
                    bT[:, poc:poc + 1], None, ALU.add)],
                    [f"p_S{poc % 2}h{phalf}", "bT"], [f"sT{psp}"])
                for q in range(4):
                    xb = (2 * phalf + q) % 4
                    dma("gp", xp[xb][:, :],
                        x_h.ap()[phalf * 512 + q * 128: phalf * 512 + (q + 1) * 128,
                                 poc * 128:(poc + 1) * 128],
                        [], [f"xp{xb}"])

            def outpiece(q):
                poc, phalf = pend
                psp = phalf
                tp = q % 2
                rv = phalf * 512 + q * 128
                xb = (2 * phalf + q) % 4
                mm([lambda e, psp=psp, q=q, tp=tp: e.transpose(
                    p_o[tp][:, :], sT[psp][:, q * 128:(q + 1) * 128], idf[:, :])],
                   [f"sT{psp}", "idf"], [f"p_o{tp}"])
                P.op("dve", [lambda e, tp=tp, xb=xb: e.tensor_add(
                    opc[xb][:, :], p_o[tp][:, :], xp[xb][:, :])],
                    [f"p_o{tp}", f"xp{xb}"], [f"opc{xb}"])
                P.op("sy", [lambda e, xb=xb, rv=rv, poc=poc: e.dma_start(
                    out=out_h.ap()[rv:rv + 128, poc * 128:(poc + 1) * 128],
                    in_=opc[xb][:, :])],
                    reads=[f"opc{xb}"], writes=[], ctr=fam_ctr(f"out{xb}"))

            for hc in range(DC):
                mm([lambda e, par=par, hc=hc, sl=sl, half=half: e.matmul(
                    p_S[par][:, sl], wo_t[par][:, hc * 128:(hc + 1) * 128],
                    qctx[:, hc * R + half * 512: hc * R + (half + 1) * 512],
                    start=(hc == 0), stop=False)],
                   [f"wo{par}", f"qctx{hc}"], [f"p_S{par}h{half}"], inc=False,
                   tokens=tuple(guard3) if (oc <= 1 and hc == 0) else ())
                if pend is not None and hc in (7, 14, 21, 28):
                    outpiece(hc // 7 - 1)
            for dc in range(DC):
                mm([lambda e, par=par, dc=dc, sl=sl, half=half: e.matmul(
                    p_S[par][:, sl], wfT[par][:, dc * 128:(dc + 1) * 128],
                    xnT2[:, dc * R + half * 512: dc * R + (half + 1) * 512],
                    start=False, stop=False)],
                   [f"wfT{par}", "xnT2"], [f"p_S{par}h{half}"], inc=False)
            mm([lambda e, par=par, oc=oc, sl=sl, half=half: e.matmul(
                p_S[par][:, sl], BT_s[:, oc * 128:(oc + 1) * 128],
                u_m[:, half * 512:(half + 1) * 512], start=False, stop=True)],
               ["BT_s", "u_m"], [f"p_S{par}h{half}"])
            pend = (oc, half)

    poc, phalf = pend
    psp = phalf
    P.op("dve", [lambda e: e.tensor_scalar(
        sT[psp][:, :], p_S[poc % 2][:, phalf * 512:(phalf + 1) * 512],
        bT[:, poc:poc + 1], None, ALU.add)],
        [f"p_S{poc % 2}h{phalf}", "bT"], [f"sT{psp}"])
    for q in range(4):
        tp = q % 2
        rv = phalf * 512 + q * 128
        xb = (2 * phalf + q) % 4
        dma("gp", xp[xb][:, :],
            x_h.ap()[rv:rv + 128, poc * 128:(poc + 1) * 128], [], [f"xp{xb}"])
        mm([lambda e, q=q, tp=tp: e.transpose(
            p_o[tp][:, :], sT[psp][:, q * 128:(q + 1) * 128], idf[:, :])],
           [f"sT{psp}", "idf"], [f"p_o{tp}"])
        P.op("dve", [lambda e, tp=tp, xb=xb: e.tensor_add(
            opc[xb][:, :], p_o[tp][:, :], xp[xb][:, :])],
            [f"p_o{tp}", f"xp{xb}"], [f"opc{xb}"])
        P.op("sy", [lambda e, xb=xb, rv=rv: e.dma_start(
            out=out_h.ap()[rv:rv + 128, poc * 128:(poc + 1) * 128],
            in_=opc[xb][:, :])],
            reads=[f"opc{xb}"], writes=[], ctr=fam_ctr(f"out{xb}"))

    for xb in range(4):
        key = "f_" + f"out{xb}"
        semh, val, _ = P.ctr[key]
        P.lists["sy"].append(lambda e, s=semh, v=val: e.wait_ge(s, v))

    with nc.Block() as block:
        @block.tensor
        def _(t):
            for f in P.lists["pe"]:
                f(t)

        @block.scalar
        def _(s):
            for f in P.lists["act"]:
                f(s)

        @block.vector
        def _(v):
            for f in P.lists["dve"]:
                f(v)

        @block.gpsimd
        def _(g):
            for f in P.lists["gp"]:
                f(g)

        @block.sync
        def _(sy):
            for f in P.lists["sy"]:
                f(sy)

    p3ps.close()
    p3stk.close()
    gstk.close()
    return nc


def kernel(L0_attn_hidden_states, L0_attn_Wq, L0_attn_Wo, L0_attn_kv_cache,
           L0_attn_block_table, L0_ffn_W, L0_ffn_b, L0_ffn_A, L0_ffn_B,
           L0_ffn_gate_w, L0_ffn_gate_b):
    x = np.ascontiguousarray(np.asarray(L0_attn_hidden_states, dtype=np.float32)).reshape(8192, 4096)
    slab = lambda W: np.ascontiguousarray(
        W.reshape(32, 128, 32, 128).transpose(2, 1, 0, 3).reshape(4096, 4096))
    wq = slab(np.asarray(L0_attn_Wq, dtype=np.float32))
    wo = slab(np.asarray(L0_attn_Wo, dtype=np.float32))
    wf = np.ascontiguousarray(np.asarray(L0_ffn_W, dtype=np.float32)
                              .reshape(32, 128, 32, 128).transpose(0, 3, 2, 1)
                              .reshape(4096, 4096))
    kvc = np.ascontiguousarray(np.asarray(L0_attn_kv_cache, dtype=np.float32))
    btb = np.ascontiguousarray(np.asarray(L0_attn_block_table, dtype=np.int32).reshape(512, 1))
    am = np.ascontiguousarray(np.asarray(L0_ffn_A, dtype=np.float32)
                              .reshape(64, 32, 128).transpose(2, 1, 0).reshape(128, 2048))
    bm = np.ascontiguousarray(np.asarray(L0_ffn_B, dtype=np.float32).T)
    gw = np.ascontiguousarray(np.asarray(L0_ffn_gate_w, dtype=np.float32)
                              .reshape(4, 32, 128).transpose(2, 1, 0).reshape(128, 128))
    gb = np.ascontiguousarray(np.asarray(L0_ffn_gate_b, dtype=np.float32).reshape(1, 4))
    bv = np.ascontiguousarray(np.asarray(L0_ffn_b, dtype=np.float32).reshape(32, 128).T)

    idf = np.eye(128, dtype=np.float32)
    idb = np.eye(128, dtype=np.float32).astype(ml_dtypes.bfloat16)
    iota = np.arange(64, dtype=np.float32).reshape(64, 1)
    inv = 1.0 / 10000.0 ** (np.arange(0, 128, 2, dtype=np.float32) / 128.0)
    ang = np.arange(4096, dtype=np.float32)[:, None] * inv[None, :]
    emb = np.concatenate([ang, ang], axis=1)
    cos_t = np.ascontiguousarray(np.cos(emb).astype(np.float32).T)
    sin_t = np.sin(emb).astype(np.float32)
    sin_t[:, :64] *= -1.0
    sin_t = np.ascontiguousarray(sin_t.T)

    nc = build()
    in_maps = []
    for c in range(8):
        s0 = (c * R) % 4096
        in_maps.append({
            "x": np.ascontiguousarray(x[c * R:(c + 1) * R]),
            "wq": wq, "wo": wo, "wf": wf, "kvc": kvc, "btb": btb,
            "am": am, "bm": bm, "gw": gw, "gbrow": gb, "bvec": bv,
            "cosT": np.ascontiguousarray(cos_t[:, s0:s0 + R]),
            "sinT": np.ascontiguousarray(sin_t[:, s0:s0 + R]),
            "idf": idf, "idb": idb, "iotac": iota,
        })
    res = run_bass_kernel_spmd(nc, in_maps, core_ids=list(range(8)))
    out = np.concatenate([np.asarray(res.results[c]["out"]) for c in range(8)], axis=0)
    return out.reshape(2, 4096, 4096)
